# version 11
# baseline (speedup 1.0000x reference)
import contextlib
import numpy as np
import ml_dtypes
import concourse.bass as bass
import concourse.mybir as mybir
from concourse.bass_utils import run_bass_kernel_spmd

F32 = mybir.dt.float32
BF16 = mybir.dt.bfloat16
AF = mybir.ActivationFunctionType
ALU = mybir.AluOpType
AX = mybir.AxisListType
bf16 = ml_dtypes.bfloat16

NCORES = 8
SEQS = 2
T = 4096
C = 1024
KC = 8
TC = 512
TI = TC // 128
NCHUNK = SEQS * T // TC
IF = 4096
IFC = IF // 128
G = 2816
GC = G // 128
LN_EPS = 1e-5
WKV_EPS = 1e-6
E30 = float(np.float32(np.exp(30.0)))
M1 = float(np.float32(1.5 * 2**22))
M2 = float(np.float32(1.5 * 2**23))
M3 = float(np.float32(1.5 * 2**24))
NS_SLICES = [(0, 512), (512, 512), (1024, 512), (1536, 512), (2048, 512), (2560, 256)]
PAD_CC = 0
SETTLES = False

ENGINES = ("sync", "tensor", "vector", "scalar", "gpsimd")
STREAM_ENGINE = {"te": "tensor", "ve": "vector", "se": "scalar", "ge": "gpsimd",
                 "syd": "sync", "scd": "scalar", "gpd": "gpsimd"}
STREAM_STEP = {"te": 1, "ve": 1, "se": 1, "ge": 1, "syd": 16, "scd": 16, "gpd": 16}


class Buf:
    __slots__ = ("name", "writer", "readers")

    def __init__(self, name):
        self.name = name
        self.writer = None
        self.readers = []


class Op:
    __slots__ = ("id", "stream", "engine", "sem", "emit", "deps", "label",
                 "cost", "count")

    def __init__(self, id, stream, engine, sem, emit, deps, label, cost):
        self.id = id
        self.stream = stream
        self.engine = engine
        self.sem = sem
        self.emit = emit
        self.deps = deps
        self.label = label
        self.cost = cost
        self.count = None


DEF_COST = {"te": 0.18, "ve": 0.55, "se": 0.45, "ge": 0.9,
            "syd": 1.3, "scd": 1.3, "gpd": 1.3}
DMA_ISSUE = 0.3


class Prog:
    def __init__(self):
        self.ops = []
        self.sems = {}
        self.sem_alloc = None
        self.last_dma_write = {}
        self.sched = None

    def _sem(self, name):
        if name not in self.sems:
            self.sems[name] = self.sem_alloc(name)
        return self.sems[name]

    def do(self, stream, emit, reads=(), writes=(), cost=None):
        is_dma = stream in ("syd", "scd", "gpd")
        oid = len(self.ops)
        if is_dma:
            sem = "B_" + (writes[0].name if writes else "misc")
        else:
            sem = stream
        self._sem(sem)
        deps = set()
        for b in reads:
            if b.writer is not None:
                deps.add(b.writer)
        for b in writes:
            if b.writer is not None:
                w = self.ops[b.writer]
                if not (is_dma and w.sem.startswith("B_") and not b.readers):
                    deps.add(b.writer)
            deps.update(b.readers)
        if is_dma and sem in self.last_dma_write:
            p = self.last_dma_write[sem]
            assert self.ops[p].engine == STREAM_ENGINE[stream], \
                f"buf {sem} written from two queues"
            deps.add(p)
        import sys as _sys
        lbl = f"L{_sys._getframe(1).f_lineno}"
        op = Op(oid, stream, STREAM_ENGINE[stream], sem, emit, deps, lbl,
                DEF_COST[stream] if cost is None else cost)
        self.ops.append(op)
        if is_dma:
            self.last_dma_write[sem] = oid
        for b in reads:
            b.readers.append(oid)
        for b in writes:
            b.writer = oid
            b.readers = []
        return oid

    def schedule(self):
        import heapq
        n = len(self.ops)
        succ = [[] for _ in range(n)]
        indeg = [0] * n
        for op in self.ops:
            for p in op.deps:
                succ[p].append(op.id)
                indeg[op.id] += 1
        prio = [0.0] * n
        for i in range(n - 1, -1, -1):
            m = 0.0
            for s_ in succ[i]:
                if prio[s_] > m:
                    m = prio[s_]
            prio[i] = self.ops[i].cost + m
        npred_done = [0] * n
        ready_t = [0.0] * n
        finish = [0.0] * n
        eng_free = {e: 0.0 for e in ENGINES}
        heap = [(-prio[i], i) for i in range(n) if indeg[i] == 0]
        heapq.heapify(heap)
        order = {e: [] for e in ENGINES}
        scheduled = 0
        while heap:
            _, i = heapq.heappop(heap)
            op = self.ops[i]
            start = max(eng_free[op.engine], ready_t[i])
            if op.sem.startswith("B_"):
                eng_free[op.engine] = start + DMA_ISSUE
                finish[i] = start + DMA_ISSUE + op.cost
            else:
                finish[i] = start + op.cost
                eng_free[op.engine] = finish[i]
            order[op.engine].append(op)
            scheduled += 1
            for s_ in succ[i]:
                npred_done[s_] += 1
                if ready_t[s_] < finish[i]:
                    ready_t[s_] = finish[i]
                if npred_done[s_] == indeg[s_]:
                    heapq.heappush(heap, (-prio[s_], s_))
        assert scheduled == n, f"cycle: scheduled {scheduled}/{n}"
        cnt = {}
        for e in ENGINES:
            for op in order[e]:
                step = 16 if op.sem.startswith("B_") else 1
                cnt[op.sem] = cnt.get(op.sem, 0) + step
                op.count = cnt[op.sem]
        self.sched = order
        self.cnt = cnt
        self.makespan = max(eng_free.values())
        return order

    def replay(self, engine, eng, log=None):
        if self.sched is None:
            self.schedule()
        hw = {}
        for op in self.sched[engine]:
            is_dma = op.stream in ("syd", "scd", "gpd")
            waits = {}
            for p in op.deps:
                po = self.ops[p]
                if po.sem == "te" and op.sem == "te":
                    continue
                if waits.get(po.sem, 0) < po.count:
                    waits[po.sem] = po.count
            prod_engines = set()
            for s, c in sorted(waits.items()):
                if hw.get(s, 0) >= c:
                    continue
                eng.wait_ge(self.sems[s], c)
                if not s.startswith("B_"):
                    prod_engines.add(STREAM_ENGINE[s])
                if log is not None:
                    log.write(f"{engine}: wait {s}>={c}\n")
                hw[s] = c
            if PAD_CC and prod_engines and (is_dma or (prod_engines - {engine})):
                eng.nop(cycle_cnt=PAD_CC)
            inst = op.emit(eng)
            inst.then_inc(self.sems[op.sem], 16 if is_dma else 1)
            if log is not None:
                log.write(f"{engine}: [{op.sem}:{op.count}] {op.label}\n")


def build_program(stages=("A", "B", "C"), n_chunks=NCHUNK, taps=()):
    nc = bass.Bass("TRN2", target_bir_lowering=False)
    P = Prog()

    x0 = nc.dram_tensor("x0", [SEQS * T, C], F32, kind="ExternalInput")
    wkT = nc.dram_tensor("wkT", [C, C], F32, kind="ExternalInput")
    wvT = nc.dram_tensor("wvT", [C, C], F32, kind="ExternalInput")
    wrT = nc.dram_tensor("wrT", [C, C], F32, kind="ExternalInput")
    woT = nc.dram_tensor("woT", [C, C], F32, kind="ExternalInput")
    ffkT = nc.dram_tensor("ffkT", [C, IF], F32, kind="ExternalInput")
    ffvT = nc.dram_tensor("ffvT", [IF, C], F32, kind="ExternalInput")
    wg = nc.dram_tensor("wg", [G, C], F32, kind="ExternalInput")
    wu = nc.dram_tensor("wu", [G, C], F32, kind="ExternalInput")
    wd = nc.dram_tensor("wd", [C, G], F32, kind="ExternalInput")
    td = nc.dram_tensor("td", [C, 1], F32, kind="ExternalInput")
    tfin = nc.dram_tensor("tf", [C, 1], F32, kind="ExternalInput")
    g1d = nc.dram_tensor("g1", [C, 1], F32, kind="ExternalInput")
    b1d = nc.dram_tensor("b1", [C, 1], F32, kind="ExternalInput")
    g2d = nc.dram_tensor("g2", [C, 1], F32, kind="ExternalInput")
    b2d = nc.dram_tensor("b2", [C, 1], F32, kind="ExternalInput")
    yo = nc.dram_tensor("y", [SEQS * T, C], F32, kind="ExternalOutput")
    c_kvr = nc.dram_tensor("c_kvr", [3, KC, KC, 128, 128], BF16)
    c_o = nc.dram_tensor("c_o", [KC, 128, C], BF16)
    c_ffk = nc.dram_tensor("c_ffk", [IFC, KC, 128, 128], BF16)
    c_ffv = nc.dram_tensor("c_ffv", [IFC, 128, C], BF16)
    c_qg = nc.dram_tensor("c_qg", [KC, 128, G], BF16)
    c_qu = nc.dram_tensor("c_qu", [KC, 128, G], BF16)
    c_qd = nc.dram_tensor("c_qd", [GC, 128, C], BF16)
    tap_d = {}
    for tname, tshape, tdt in taps:
        tap_d[tname] = nc.dram_tensor("tap_" + tname, tshape, tdt, kind="ExternalOutput")

    with contextlib.ExitStack() as ctx:
        E = ctx.enter_context

        def sb(name, shape, dt):
            t = E(nc.sbuf_tensor(name, shape, dt))
            return t, Buf(name)

        def ps(name, shape):
            t = E(nc.psum_tensor(name, shape, F32))
            return t, Buf(name)

        FA, bFA = sb("FA", [128, 4096], F32)
        FB, bFB = sb("FB", [128, 4096], F32)
        FC, bFC = sb("FC", [128, 4096], F32)
        FD, bFD = sb("FD", [128, 4096], F32)
        XP, bXP = [], []
        for i in range(2):
            t, b = sb(f"XP{i}", [128, 1024], F32)
            XP.append(t); bXP.append(b)
        SGP, bSGP = sb("SGP", [128, 512], F32)
        HPPs = []
        for i in range(2):
            t, b = sb(f"HPP{i}", [128, 512], F32)
            HPPs.append((t, b))

        def mk_qset(tag, W):
            xnp, bxnp = sb("XNP" + tag, [128, W], F32)
            rndp, brndp = sb("RNDP" + tag, [128, W], F32)
            qt, bqt = [], []
            for i in range(6):
                t, b = sb(f"QT{tag}{i}", [128, W], BF16)
                qt.append(t); bqt.append(b)
            nbm = W // 16
            maxa, bmaxa = sb("MAXA" + tag, [128, nbm], F32)
            scl, bscl = sb("SCL" + tag, [128, nbm], F32)
            rscl, brscl = sb("RSCL" + tag, [128, nbm], F32)
            return dict(XNP=xnp, bXNP=bxnp, RNDP=rndp, bRNDP=brndp,
                        QT=qt, bQT=bqt, MAXA=maxa, bMAXA=bmaxa,
                        SCL=scl, bSCL=bscl, RSCL=rscl, bRSCL=brscl, W=W)

        QS0 = mk_qset("a", 1024)
        QS1 = mk_qset("b", 512)
        QSH = [QS0, QS1]
        XLF, bXLF = sb("XLF", [128, 4096], BF16)
        bXLFk = [Buf(f"XLFk{kc}") for kc in range(KC)]
        SRB, bSRB = sb("SRB", [128, 4096], BF16)
        RWKVB, bRWKVB = sb("RWKVB", [128, 4096], BF16)
        RUSQ, bRUSQ = sb("RUSQ", [128, IFC * 512], BF16)
        QP, bQP = [], []
        for i in range(2):
            t, b = sb(f"QP{i}", [128, 1024], BF16)
            QP.append(t); bQP.append(b)
        QP2, bQP2 = sb("QP2", [128, 512], BF16)
        QPH = [(QP[0], bQP[0]), (QP2, bQP2)]
        QHT, bQHTs = [], []
        for i in range(2):
            t, b = sb(f"QHT{i}", [128, 512], BF16)
            QHT.append(t); bQHTs.append(b)
        QRGU, bQRGU = sb("QRGU", [128, KC * 512], BF16)
        QRD, bQRD = sb("QRD", [128, 8 * 512], BF16)
        SGQ, bSGQ = sb("SGQ", [128, TI * 512], F32)
        LHS, bLHS = [], []
        for i in range(3):
            t, b = sb(f"LHS{i}", [128, 1024], BF16)
            LHS.append(t); bLHS.append(b)
        RHS, bRHS = [], []
        for i in range(4):
            t, b = sb(f"RHS{i}", [128, 512], BF16)
            RHS.append(t); bRHS.append(b)
        BN6, bBN6 = sb("BN6", [128, 12], F32)
        MVA, bMVA = sb("MVA", [128, 8], F32)
        MVB, bMVB = sb("MVB", [128, 8], F32)
        MVCc, bMVC = sb("MVC", [128, 8], F32)
        MVD, bMVD = sb("MVD", [128, 8], F32)
        R0, bR0 = sb("R0", [128, 4], F32)
        R1, bR1 = sb("R1", [128, 4], F32)
        R2, bR2 = sb("R2", [128, 4], F32)
        R3, bR3 = sb("R3", [128, 4], F32)
        TMPS, bTMPS = sb("TMPS", [128, 4], F32)
        G1S, bG1S = sb("G1S", [128, 8], F32)
        B1S, bB1S = sb("B1S", [128, 8], F32)
        G2S, bG2S = sb("G2S", [128, 8], F32)
        B2S, bB2S = sb("B2S", [128, 8], F32)
        B1B, bB1B = sb("B1B", [128, 8], BF16)
        B2B, bB2B = sb("B2B", [128, 8], BF16)
        DEC, bDEC = sb("DEC", [128, 8], F32)
        TFE, bTFE = sb("TFE", [128, 8], F32)
        BK, bBK = sb("BK", [128, 8], F32)
        BV, bBV = sb("BV", [128, 8], F32)
        BR, bBR = sb("BR", [128, 8], F32)
        BU, bBU = sb("BU", [128, 32], F32)
        CNUM, bCNUM = sb("CNUM", [128, 8], F32)
        CDEN, bCDEN = sb("CDEN", [128, 8], F32)
        TTILE, bTTILE = sb("TTILE", [128, 128], BF16)
        CONSTS, bCONSTS = sb("CONSTS", [128, 2], F32)
        if SETTLES:
            ZB, bZB = sb("ZB", [128, 512], BF16)
            SETL, bSETL = sb("SETL", [128, 1], F32)
            SETL2, bSETL2 = sb("SETL2", [128, 1], F32)
        PB, bPB = [], []
        for i in range(8):
            t, b = ps(f"PB{i}", [128, 512])
            PB.append(t); bPB.append(b)
        bx0 = Buf("x0")
        bc_kvr = Buf("c_kvr"); bc_o = Buf("c_o"); bc_ffk = Buf("c_ffk")
        bc_ffv = Buf("c_ffv"); bc_qg = Buf("c_qg"); bc_qu = Buf("c_qu"); bc_qd = Buf("c_qd")
        by = Buf("y")

        P.sem_alloc = lambda name: E(nc.semaphore("sem_" + name))

        def stripe(tile, s, w=512):
            return tile[:, s * w:(s + 1) * w]

        def emit_quant(src_ap, out_ap, W, rbufs, wbufs, qs=None):
            if qs is None:
                qs = QS0
            assert W <= qs["W"]
            nb = W // 16
            XNP, bXNP = qs["XNP"], qs["bXNP"]
            RNDP, bRNDP = qs["RNDP"], qs["bRNDP"]
            QT, bQT = qs["QT"], qs["bQT"]
            MAXA, bMAXA = qs["MAXA"], qs["bMAXA"]
            SCL, bSCL = qs["SCL"], qs["bSCL"]
            RSCL, bRSCL = qs["RSCL"], qs["bRSCL"]
            blk = lambda ap: ap.rearrange("p (nb s) -> p nb s", s=16)
            P.do("ve", lambda v: v.tensor_reduce(
                MAXA[:, 0:nb], blk(src_ap), axis=AX.X, op=ALU.max,
                apply_absolute_value=True), reads=rbufs, writes=[bMAXA])
            P.do("ve", lambda v: v.tensor_scalar(
                SCL[:, 0:nb], MAXA[:, 0:nb], 1.0 / 6.0, 1e-12, op0=ALU.mult, op1=ALU.max),
                reads=[bMAXA], writes=[bSCL], cost=0.1)
            P.do("ve", lambda v: v.reciprocal(RSCL[:, 0:nb], SCL[:, 0:nb]),
                 reads=[bSCL], writes=[bRSCL], cost=0.1)
            rs_b = RSCL[:, 0:nb].unsqueeze(2).broadcast_to([128, nb, 16])
            sc_b = SCL[:, 0:nb].unsqueeze(2).broadcast_to([128, nb, 16])
            P.do("ve", lambda v: v.tensor_tensor(
                blk(XNP[:, 0:W]), blk(src_ap), rs_b, op=ALU.mult),
                reads=rbufs + [bRSCL], writes=[bXNP])
            P.do("ve", lambda v: v.tensor_scalar(
                RNDP[:, 0:W], XNP[:, 0:W], M1, M1 - 2.0, op0=ALU.add, op1=ALU.max),
                reads=[bXNP], writes=[bRNDP])
            P.do("ve", lambda v: v.tensor_scalar(
                QT[0][:, 0:W], RNDP[:, 0:W], M1 + 2.0, M1, op0=ALU.min, op1=ALU.subtract),
                reads=[bRNDP], writes=[bQT[0]])
            P.do("ve", lambda v: v.tensor_scalar(
                RNDP[:, 0:W], XNP[:, 0:W], M2, M2 - 4.0, op0=ALU.add, op1=ALU.max),
                reads=[bXNP], writes=[bRNDP])
            P.do("ve", lambda v: v.tensor_scalar(
                QT[1][:, 0:W], RNDP[:, 0:W], M2 + 4.0, M2, op0=ALU.min, op1=ALU.subtract),
                reads=[bRNDP], writes=[bQT[1]])
            P.do("ge", lambda g: g.tensor_scalar(
                QT[2][:, 0:W], QT[1][:, 0:W], 2.0, -2.0, op0=ALU.min, op1=ALU.max),
                reads=[bQT[1]], writes=[bQT[2]])
            P.do("se", lambda s: s.activation(
                RNDP[:, 0:W], XNP[:, 0:W], AF.Copy, bias=M3),
                reads=[bXNP], writes=[bRNDP])
            P.do("se", lambda s: s.activation(
                QT[3][:, 0:W], RNDP[:, 0:W], AF.Copy, bias=-M3),
                reads=[bRNDP], writes=[bQT[3]])
            P.do("ge", lambda g: g.tensor_scalar(
                QT[4][:, 0:W], QT[3][:, 0:W], 4.0, -4.0, op0=ALU.min, op1=ALU.max),
                reads=[bQT[3]], writes=[bQT[4]])
            P.do("ge", lambda g: g.tensor_tensor(
                QT[5][:, 0:W], QT[1][:, 0:W], QT[2][:, 0:W], op=ALU.subtract),
                reads=[bQT[1], bQT[2]], writes=[bQT[5]])
            P.do("ge", lambda g: g.tensor_tensor(
                QT[2][:, 0:W], QT[3][:, 0:W], QT[4][:, 0:W], op=ALU.subtract),
                reads=[bQT[3], bQT[4]], writes=[bQT[2]])
            P.do("ve", lambda v: v.tensor_tensor(
                QT[3][:, 0:W], QT[0][:, 0:W], QT[5][:, 0:W], op=ALU.add),
                reads=[bQT[0], bQT[5]], writes=[bQT[3]])
            P.do("ve", lambda v: v.tensor_tensor(
                QT[4][:, 0:W], QT[3][:, 0:W], QT[2][:, 0:W], op=ALU.add),
                reads=[bQT[3], bQT[2]], writes=[bQT[4]])
            P.do("ve", lambda v: v.tensor_tensor(
                blk(out_ap), blk(QT[4][:, 0:W]), sc_b, op=ALU.mult),
                reads=[bQT[4], bSCL], writes=wbufs)

        def emit_bn(src_ap, rbufs, mv_ap, bMV, ti):
            P.do("ve", lambda v: v.bn_stats(BN6[:, 0:6], src_ap[:, 0:512]),
                 reads=rbufs, writes=[bBN6])
            P.do("ve", lambda v: v.bn_stats(BN6[:, 6:12], src_ap[:, 512:1024]),
                 reads=rbufs, writes=[bBN6])
            P.do("ve", lambda v: v.bn_aggr(mv_ap, BN6[:, 0:12]),
                 reads=[bBN6], writes=[bMV])

        def emit_rstd(mv, bMV, r_t, bR, ti):
            var_ap = mv[:, 2 * ti + 1:2 * ti + 2]
            P.do("se", lambda s: s.activation(TMPS[:, ti:ti + 1], var_ap, AF.Ln,
                                              bias=CONSTS[:, 0:1]),
                 reads=[bMV, bCONSTS], writes=[bTMPS])
            P.do("se", lambda s: s.activation(r_t[:, ti:ti + 1], TMPS[:, ti:ti + 1], AF.Exp, scale=-0.5),
                 reads=[bTMPS], writes=[bR])

        def mm_settle(junk_idx, acc=False, after_bufs=()):
            if not SETTLES:
                return Buf("dummy")
            for r in range(1):
                P.do("te", lambda t: t.matmul(
                    PB[junk_idx][:, :], ZB[:, 0:128], ZB[:, 0:512],
                    start=not acc, stop=not acc, skip_group_check=True),
                    reads=[bZB] + list(after_bufs), writes=[bPB[junk_idx]])
            return bPB[junk_idx]

        def ve_settle(after_bufs=()):
            if not SETTLES:
                return Buf("dummy")
            P.do("ve", lambda v: v.memset(SETL[:, :], 0.0),
                 reads=list(after_bufs), writes=[bSETL])
            return bSETL

        def se_settle(after_bufs=()):
            if not SETTLES:
                return Buf("dummy")
            P.do("se", lambda s: s.copy(SETL2[:, :], SETL[:, :]),
                 reads=list(after_bufs), writes=[bSETL2])
            return bSETL2

        def tap(name, src_ap, rbufs):
            if name in tap_d:
                dst = tap_d[name]
                P.do("syd", lambda s, d=dst, a=src_ap: s.dma_start(d[:, :], a), reads=rbufs)

        if SETTLES:
            P.do("ve", lambda v: v.memset(ZB[:, :], 0.0), writes=[bZB])
            P.do("ve", lambda v: v.memset(SETL[:, :], 0.0), writes=[bSETL])
        P.do("ve", lambda v: v.memset(CONSTS[:, 0:1], LN_EPS), writes=[bCONSTS])
        P.do("ve", lambda v: v.memset(CONSTS[:, 1:2], WKV_EPS), writes=[bCONSTS])
        for (dr, tile, b) in ((td, DEC, bDEC), (tfin, TFE, bTFE), (g1d, G1S, bG1S),
                              (b1d, B1S, bB1S), (g2d, G2S, bG2S), (b2d, B2S, bB2S)):
            for kc in range(KC):
                P.do("gpd", lambda g, dr=dr, tile=tile, kc=kc: g.dma_start(
                    tile[:, kc:kc + 1], dr[kc * 128:(kc + 1) * 128, 0:1]), writes=[b])
        P.do("se", lambda s: s.activation(DEC[:, :], DEC[:, :], AF.Exp), reads=[bDEC], writes=[bDEC])
        P.do("se", lambda s: s.activation(DEC[:, :], DEC[:, :], AF.Exp, scale=-1.0),
             reads=[bDEC], writes=[bDEC])
        P.do("se", lambda s: s.activation(TFE[:, :], TFE[:, :], AF.Exp), reads=[bTFE], writes=[bTFE])
        P.do("ve", lambda v: v.tensor_copy(B1B[:, :], B1S[:, :]), reads=[bB1S], writes=[bB1B])
        P.do("ve", lambda v: v.tensor_copy(B2B[:, :], B2S[:, :]), reads=[bB2S], writes=[bB2B])

        for qi, (wsrc, bbias) in enumerate(((wkT, bBK), (wvT, bBV), (wrT, bBR))):
            bias_tile = (BK, BV, BR)[qi]
            for kc in range(KC):
                P.do("syd", lambda s, wsrc=wsrc, kc=kc: s.dma_start(
                    FA[:, 0:1024], wsrc[kc * 128:(kc + 1) * 128, :]), writes=[bFA])
                P.do("ve", lambda v, kc=kc: v.tensor_scalar(
                    XLF[:, 0:1024], FA[:, 0:1024], G1S[:, kc:kc + 1], None,
                    op0=ALU.mult, op1=ALU.bypass), reads=[bFA, bG1S], writes=[bXLF] + bXLFk)
                P.do("ve", lambda v: v.tensor_copy(SRB[:, 0:1024], FA[:, 0:1024]),
                     reads=[bFA], writes=[bSRB])
                bvs = ve_settle([bSRB, bB1B])
                for m in range(KC):
                    P.do("te", lambda t, m=m, kc=kc: t.matmul(
                        PB[7][:, m:m + 1], SRB[:, m * 128:(m + 1) * 128],
                        B1B[:, kc:kc + 1], start=(kc == 0 and m == 0),
                        stop=(kc == KC - 1 and m == KC - 1)),
                        reads=[bSRB, bB1B, bvs], writes=[bPB[7]])
                P.do("syd", lambda s, qi=qi, kc=kc: s.dma_start(
                    bass.AP(c_kvr, ((qi * KC) * KC + kc) * 128 * 128,
                            [[128, 128], [KC * 128 * 128, KC], [1, 128]]),
                    XLF[:, 0:1024].rearrange("p (m c) -> p m c", m=KC)),
                    reads=[bXLF] + bXLFk, writes=[bc_kvr])
            bts = mm_settle(6)
            P.do("ve", lambda v, bias_tile=bias_tile: v.tensor_copy(bias_tile[:, :], PB[7][:, 0:KC]),
                 reads=[bPB[7], bts], writes=[bbias])
        for kc in range(KC):
            P.do("syd", lambda s, kc=kc: s.dma_start(
                FA[:, 0:1024], woT[kc * 128:(kc + 1) * 128, :]), writes=[bFA])
            P.do("ve", lambda v: v.tensor_copy(XLF[:, 0:1024], FA[:, 0:1024]),
                 reads=[bFA], writes=[bXLF] + bXLFk)
            P.do("syd", lambda s, kc=kc: s.dma_start(
                c_o[kc, :, :], XLF[:, 0:1024]), reads=[bXLF] + bXLFk, writes=[bc_o])
        for kc in range(KC):
            P.do("syd", lambda s, kc=kc: s.dma_start(
                FA[:, :], ffkT[kc * 128:(kc + 1) * 128, :]), writes=[bFA])
            P.do("ve", lambda v, kc=kc: v.tensor_scalar(
                XLF[:, :], FA[:, :], G2S[:, kc:kc + 1], None,
                op0=ALU.mult, op1=ALU.bypass), reads=[bFA, bG2S], writes=[bXLF] + bXLFk)
            P.do("ve", lambda v: v.tensor_copy(SRB[:, :], FA[:, :]), reads=[bFA], writes=[bSRB])
            bvs2 = ve_settle([bSRB, bB2B])
            for m in range(IFC):
                P.do("te", lambda t, m=m, kc=kc: t.matmul(
                    PB[7][:, m:m + 1], SRB[:, m * 128:(m + 1) * 128],
                    B2B[:, kc:kc + 1], start=(kc == 0 and m == 0),
                    stop=(kc == KC - 1 and m == IFC - 1)),
                    reads=[bSRB, bB2B, bvs2], writes=[bPB[7]])
            P.do("syd", lambda s, kc=kc: s.dma_start(
                bass.AP(c_ffk, kc * 128 * 128, [[128, 128], [KC * 128 * 128, IFC], [1, 128]]),
                XLF[:, :].rearrange("p (m c) -> p m c", m=IFC)),
                reads=[bXLF] + bXLFk, writes=[bc_ffk])
        bts2 = mm_settle(6)
        P.do("ve", lambda v: v.tensor_copy(BU[:, :], PB[7][:, 0:IFC]), reads=[bPB[7], bts2], writes=[bBU])
        for kc in range(IFC):
            P.do("syd", lambda s, kc=kc: s.dma_start(
                FA[:, 0:1024], ffvT[kc * 128:(kc + 1) * 128, :]), writes=[bFA])
            P.do("ve", lambda v: v.tensor_copy(XLF[:, 0:1024], FA[:, 0:1024]),
                 reads=[bFA], writes=[bXLF] + bXLFk)
            P.do("syd", lambda s, kc=kc: s.dma_start(
                c_ffv[kc, :, :], XLF[:, 0:1024]), reads=[bXLF] + bXLFk, writes=[bc_ffv])
        for (wsrc, cdst, bdst, mt_n, kcn) in ((wg, c_qg, bc_qg, GC, KC),
                                              (wu, c_qu, bc_qu, GC, KC)):
            for mt in range(mt_n):
                P.do("syd", lambda s, wsrc=wsrc, mt=mt: s.dma_start(
                    FA[:, 0:1024], wsrc[mt * 128:(mt + 1) * 128, :]), writes=[bFA])
                emit_quant(FA[:, 0:1024], XLF[:, 0:1024], 1024, [bFA], [bXLF] + bXLFk)
                for kc in range(kcn):
                    P.do("scd", lambda s, kc=kc: s.dma_start(
                        TTILE[:, :], XLF[:, kc * 128:(kc + 1) * 128], transpose=True),
                        reads=[bXLF] + bXLFk, writes=[bTTILE])
                    P.do("scd", lambda s, cdst=cdst, kc=kc, mt=mt: s.dma_start(
                        cdst[kc, :, mt * 128:(mt + 1) * 128], TTILE[:, :]),
                        reads=[bTTILE], writes=[bdst])
        for mt in range(KC):
            P.do("syd", lambda s, mt=mt: s.dma_start(
                FA[:, 0:G], wd[mt * 128:(mt + 1) * 128, :]), writes=[bFA])
            for qtr in range(4):
                emit_quant(FA[:, qtr * 704:(qtr + 1) * 704], XLF[:, qtr * 704:(qtr + 1) * 704],
                           704, [bFA], [bXLF] + bXLFk)
            for kc in range(GC):
                P.do("scd", lambda s, kc=kc: s.dma_start(
                    TTILE[:, :], XLF[:, kc * 128:(kc + 1) * 128], transpose=True),
                    reads=[bXLF] + bXLFk, writes=[bTTILE])
                P.do("scd", lambda s, kc=kc, mt=mt: s.dma_start(
                    c_qd[kc, :, mt * 128:(mt + 1) * 128], TTILE[:, :]),
                    reads=[bTTILE], writes=[bc_qd])
        for ch in range(n_chunks):
            seq = ch // (T // TC)
            lc = ch % (T // TC)
            t0 = seq * T + lc * TC

            for ti in range(TI):
                xp, bxp = XP[ti % 2], bXP[ti % 2]
                P.do("gpd", lambda g, ti=ti, t0=t0, xp=xp: g.dma_start(
                    xp[:, :], x0[t0 + ti * 128:t0 + (ti + 1) * 128, :]), writes=[bxp])
                emit_bn(xp, [bxp], MVA[:, 2 * ti:2 * ti + 2], bMVA, ti)
                emit_rstd(MVA, bMVA, R0, bR0, ti)
                P.do("se", lambda s, ti=ti: s.activation(
                    MVB[:, 2 * ti:2 * ti + 1], TMPS[:, ti:ti + 1], AF.Exp, scale=-1.0),
                    reads=[bTMPS], writes=[bMVB], cost=0.1)
                P.do("ve", lambda v, ti=ti: v.tensor_tensor(
                    MVB[:, 2 * ti + 1:2 * ti + 2], MVA[:, 2 * ti + 1:2 * ti + 2],
                    MVB[:, 2 * ti:2 * ti + 1], op=ALU.mult),
                    reads=[bMVA, bMVB], writes=[bMVB], cost=0.1)
                P.do("se", lambda s, ti=ti: s.activation(
                    TMPS[:, ti:ti + 1], MVB[:, 2 * ti + 1:2 * ti + 2], AF.Ln,
                    bias=CONSTS[:, 0:1]), reads=[bMVB, bCONSTS], writes=[bTMPS], cost=0.1)
                P.do("se", lambda s, ti=ti: s.activation(
                    R1[:, ti:ti + 1], TMPS[:, ti:ti + 1], AF.Exp, scale=-0.5),
                    reads=[bTMPS], writes=[bR1], cost=0.1)
                P.do("ve", lambda v, ti=ti: v.tensor_tensor(
                    MVB[:, 2 * ti:2 * ti + 1], R0[:, ti:ti + 1], R1[:, ti:ti + 1],
                    op=ALU.mult), reads=[bR0, bR1], writes=[bMVB], cost=0.1)
                P.do("ve", lambda v, ti=ti, xp=xp: v.tensor_scalar(
                    QP[ti % 2][:, :], xp[:, :],
                    MVA[:, 2 * ti:2 * ti + 1], MVB[:, 2 * ti:2 * ti + 1],
                    op0=ALU.subtract, op1=ALU.mult),
                    reads=[bxp, bMVA, bMVB], writes=[bQP[ti % 2]])
                P.do("syd", lambda s, ti=ti: s.dma_start(
                    XLF[:, :].rearrange("p (kc t) -> p kc t", t=512)[:, :, ti * 128:(ti + 1) * 128],
                    QP[ti % 2][:, :], transpose=True),
                    reads=[bQP[ti % 2]], writes=bXLFk, cost=1.0)
            tap("xlf", XLF[:, :], list(bXLFk))

            for qi in range(3):
                bias_tile, bbias = ((BK, bBK), (BV, bBV), (BR, bBR))[qi]
                for m in range(KC):
                    sl = (qi * KC + m) % 3
                    P.do("scd", lambda s, qi=qi, m=m, sl=sl: s.dma_start(
                        LHS[sl][:, :].rearrange("p (a b) -> p a b", a=KC), bass.AP(
                            c_kvr, ((qi * KC + m) * KC) * 128 * 128,
                            [[128, 128], [128 * 128, KC], [1, 128]])),
                        reads=[bc_kvr], writes=[bLHS[sl]])
                    pb, bpb = PB[m % 4], bPB[m % 4]
                    for kc in range(KC):
                        P.do("te", lambda t, pb=pb, sl=sl, kc=kc: t.matmul(
                            pb[:, :], LHS[sl][:, kc * 128:(kc + 1) * 128],
                            XLF[:, kc * 512:kc * 512 + 512],
                            start=(kc == 0), stop=(kc == KC - 1)),
                            reads=[bLHS[sl], bXLFk[kc]], writes=[bpb])
                    extra = []
                    if qi == 2 and m == KC - 1:
                        extra = [mm_settle(2)]
                    if qi == 0:
                        P.do("se", lambda s, m=m, pb=pb: s.activation(
                            stripe(FA, m), pb[:, :], AF.Exp, bias=BK[:, m:m + 1]),
                            reads=[bpb, bBK] + extra, writes=[bFA])
                    elif qi == 1:
                        P.do("ve", lambda v, m=m, pb=pb: v.tensor_scalar(
                            stripe(FB, m), pb[:, :], BV[:, m:m + 1], None,
                            op0=ALU.add, op1=ALU.bypass),
                            reads=[bpb, bBV] + extra, writes=[bFB])
                    else:
                        P.do("se", lambda s, m=m, pb=pb: s.activation(
                            stripe(SRB, m), pb[:, :], AF.Sigmoid, bias=BR[:, m:m + 1]),
                            reads=[bpb, bBR] + extra, writes=[bSRB])
            tap("ek", FA[:, :], [bFA])
            tap("vv", FB[:, :], [bFB])

            for s_ in range(KC):
                P.do("ve", lambda v, s_=s_: v.tensor_scalar(
                    stripe(FC, s_), stripe(FA, s_), E30, None, op0=ALU.min, op1=ALU.bypass),
                    reads=[bFA], writes=[bFC])
                P.do("ve", lambda v, s_=s_: v.tensor_tensor(
                    stripe(FD, s_), stripe(FC, s_), stripe(FB, s_), op=ALU.mult),
                    reads=[bFC, bFB], writes=[bFD])
                if lc == 0:
                    P.do("ve", lambda v, s_=s_: v.tensor_scalar(
                        FD[:, s_ * 512:s_ * 512 + 1], FD[:, s_ * 512:s_ * 512 + 1],
                        TFE[:, s_:s_ + 1], None, op0=ALU.mult, op1=ALU.bypass),
                        reads=[bFD, bTFE], writes=[bFD])
                    P.do("ve", lambda v, s_=s_: v.tensor_scalar(
                        FC[:, s_ * 512:s_ * 512 + 1], FC[:, s_ * 512:s_ * 512 + 1],
                        TFE[:, s_:s_ + 1], None, op0=ALU.mult, op1=ALU.bypass),
                        reads=[bFC, bTFE], writes=[bFC])
                    init_n = 0.0
                    init_d = 0.0
                else:
                    init_n = CNUM[:, s_:s_ + 1]
                    init_d = CDEN[:, s_:s_ + 1]
                dec_b = DEC[:, s_:s_ + 1].broadcast_to([128, 512])
                P.do("ve", lambda v, s_=s_, dec_b=dec_b, init_n=init_n: v.tensor_tensor_scan(
                    stripe(FA, s_), dec_b, stripe(FD, s_), init_n, op0=ALU.mult, op1=ALU.add),
                    reads=[bFD, bDEC, bCNUM], writes=[bFA])
                P.do("ve", lambda v, s_=s_, dec_b=dec_b, init_d=init_d: v.tensor_tensor_scan(
                    stripe(FB, s_), dec_b, stripe(FC, s_), init_d, op0=ALU.mult, op1=ALU.add),
                    reads=[bFC, bDEC, bCDEN], writes=[bFB])
                if lc != (T // TC) - 1:
                    P.do("ve", lambda v, s_=s_: v.tensor_copy(
                        CNUM[:, s_:s_ + 1], FA[:, s_ * 512 + 511:s_ * 512 + 512]),
                        reads=[bFA], writes=[bCNUM])
                    P.do("ve", lambda v, s_=s_: v.tensor_copy(
                        CDEN[:, s_:s_ + 1], FB[:, s_ * 512 + 511:s_ * 512 + 512]),
                        reads=[bFB], writes=[bCDEN])
                P.do("se", lambda s, s_=s_: s.activation(
                    stripe(FC, s_), stripe(FB, s_), AF.Ln, bias=CONSTS[:, 1:2]),
                    reads=[bFB, bCONSTS], writes=[bFC])
                P.do("se", lambda s, s_=s_: s.activation(
                    stripe(FD, s_), stripe(FC, s_), AF.Exp, scale=-1.0),
                    reads=[bFC], writes=[bFD])
                P.do("ve", lambda v, s_=s_: v.tensor_tensor(
                    stripe(FC, s_), stripe(FA, s_), stripe(FD, s_), op=ALU.mult),
                    reads=[bFA, bFD], writes=[bFC])
                P.do("ve", lambda v, s_=s_: v.tensor_tensor(
                    stripe(RWKVB, s_), stripe(FC, s_), stripe(SRB, s_), op=ALU.mult),
                    reads=[bFC, bSRB], writes=[bRWKVB])
            bvrwkv = ve_settle([bRWKVB])
            tap("wkv", FC[:, :], [bFC])
            tap("rwkv", RWKVB[:, :], [bRWKVB])

            if "B" not in stages:
                continue

            for ti in range(TI):
                xp, bxp = XP[ti % 2], bXP[ti % 2]
                P.do("gpd", lambda g, ti=ti, t0=t0, xp=xp: g.dma_start(
                    xp[:, :], x0[t0 + ti * 128:t0 + (ti + 1) * 128, :]), writes=[bxp])
                P.do("ve", lambda v, ti=ti, xp=xp: v.tensor_scalar(
                    FD[:, ti * 1024:(ti + 1) * 1024], xp[:, :],
                    MVA[:, 2 * ti:2 * ti + 1], R0[:, ti:ti + 1],
                    op0=ALU.subtract, op1=ALU.mult),
                    reads=[bxp, bMVA, bR0], writes=[bFD])
            tap("yd", FD[:, :], [bFD])
            for nn in range(2):
                reg, breg = (QRGU, bQRGU) if nn == 0 else (QRD, bQRD)
                for kc in range(KC):
                    P.do("scd", lambda s, kc=kc, nn=nn, reg=reg: s.dma_start(
                        reg[:, kc * 512:(kc + 1) * 512], c_o[kc, :, nn * 512:(nn + 1) * 512]),
                        reads=[bc_o], writes=[breg])
            for nn in range(2):
                pbase = 4 * nn
                reg, breg = (QRGU, bQRGU) if nn == 0 else (QRD, bQRD)
                for kc in range(KC):
                    for ti in range(TI):
                        P.do("te", lambda t, ti=ti, kc=kc, pbase=pbase, reg=reg, breg=breg: t.matmul(
                            PB[pbase + ti][:, :],
                            RWKVB[:, kc * 512 + ti * 128:kc * 512 + (ti + 1) * 128],
                            reg[:, kc * 512:(kc + 1) * 512],
                            start=(kc == 0), stop=(kc == KC - 1)),
                            reads=[bRWKVB, breg, bvrwkv], writes=[bPB[pbase + ti]])
            bos = mm_settle(7, acc=True)
            for nn in range(2):
                pbase = 4 * nn
                for ti in range(TI):
                    P.do("ve", lambda v, ti=ti, nn=nn, pbase=pbase: v.tensor_tensor(
                        FC[:, ti * 1024 + nn * 512:ti * 1024 + (nn + 1) * 512],
                        FD[:, ti * 1024 + nn * 512:ti * 1024 + (nn + 1) * 512],
                        PB[pbase + ti][:, :], op=ALU.add),
                        reads=[bFD, bPB[pbase + ti], bos], writes=[bFC])
            tap("a", FC[:, :], [bFC])

            for ti in range(TI):
                emit_bn(FC[:, ti * 1024:(ti + 1) * 1024], [bFC], MVCc[:, 2 * ti:2 * ti + 2], bMVC, ti)
                emit_rstd(MVCc, bMVC, R2, bR2, ti)
                P.do("ve", lambda v, ti=ti: v.tensor_scalar(
                    QP[ti % 2][:, :], FC[:, ti * 1024:(ti + 1) * 1024],
                    MVCc[:, 2 * ti:2 * ti + 1], R2[:, ti:ti + 1],
                    op0=ALU.subtract, op1=ALU.mult),
                    reads=[bFC, bMVC, bR2], writes=[bQP[ti % 2]])
                P.do("syd", lambda s, ti=ti: s.dma_start(
                    XLF[:, :].rearrange("p (kc t) -> p kc t", t=512)[:, :, ti * 128:(ti + 1) * 128],
                    QP[ti % 2][:, :], transpose=True),
                    reads=[bQP[ti % 2]], writes=bXLFk, cost=1.0)

            for m in range(IFC):
                sl = m % 3
                P.do("scd", lambda s, m=m, sl=sl: s.dma_start(
                    LHS[sl][:, :].rearrange("p (a b) -> p a b", a=KC), bass.AP(
                        c_ffk, (m * KC) * 128 * 128,
                        [[128, 128], [128 * 128, KC], [1, 128]])),
                    reads=[bc_ffk], writes=[bLHS[sl]])
                pb, bpb = PB[4 + m % 4], bPB[4 + m % 4]
                for kc in range(KC):
                    P.do("te", lambda t, pb=pb, sl=sl, kc=kc: t.matmul(
                        pb[:, :], LHS[sl][:, kc * 128:(kc + 1) * 128],
                        XLF[:, kc * 512:kc * 512 + 512],
                        start=(kc == 0), stop=(kc == KC - 1)),
                        reads=[bLHS[sl], bXLFk[kc]], writes=[bpb])
                extra_u = [mm_settle(0)] if m == IFC - 1 else []
                P.do("se", lambda s, m=m, pb=pb: s.activation(
                    SGP[:, :], pb[:, :], AF.Relu, bias=BU[:, m:m + 1]),
                    reads=[bpb, bBU] + extra_u, writes=[bSGP])
                P.do("se", lambda s, m=m: s.activation(
                    stripe(RUSQ, m), SGP[:, :], AF.Square),
                    reads=[bSGP], writes=[bRUSQ])
            bsrusq = se_settle([bRUSQ])
            for ti in range(TI):
                xp = FB
                P.do("gpd", lambda g, ti=ti, t0=t0: g.dma_start(
                    FB[:, ti * 1024:(ti + 1) * 1024],
                    x0[t0 + ti * 128:t0 + (ti + 1) * 128, :]), writes=[bFB])
            for nn in range(2):
                for kc in range(IFC):
                    sl = kc % 4
                    P.do("scd", lambda s, kc=kc, nn=nn, sl=sl: s.dma_start(
                        RHS[sl][:, :], c_ffv[kc, :, nn * 512:(nn + 1) * 512]),
                        reads=[bc_ffv], writes=[bRHS[sl]])
                    for ti in range(TI):
                        P.do("te", lambda t, ti=ti, kc=kc, sl=sl: t.matmul(
                            PB[ti][:, :], RUSQ[:, kc * 512 + ti * 128:kc * 512 + (ti + 1) * 128],
                            RHS[sl][:, :], start=(kc == 0), stop=(kc == IFC - 1)),
                            reads=[bRUSQ, bRHS[sl], bsrusq], writes=[bPB[ti]])
                bfs = mm_settle(4) if nn == 1 else mm_settle(5)
                for ti in range(TI):
                    P.do("ve", lambda v, ti=ti, nn=nn: v.tensor_tensor(
                        FA[:, ti * 1024 + nn * 512:ti * 1024 + (nn + 1) * 512],
                        FC[:, ti * 1024 + nn * 512:ti * 1024 + (nn + 1) * 512],
                        PB[ti][:, :], op=ALU.add),
                        reads=[bFC, bPB[ti], bfs], writes=[bFA])
                    P.do("ve", lambda v, ti=ti, nn=nn: v.tensor_tensor(
                        FC[:, ti * 1024 + nn * 512:ti * 1024 + (nn + 1) * 512],
                        FA[:, ti * 1024 + nn * 512:ti * 1024 + (nn + 1) * 512],
                        FB[:, ti * 1024 + nn * 512:ti * 1024 + (nn + 1) * 512],
                        op=ALU.add),
                        reads=[bFA, bFB], writes=[bFC])
            tap("x1", FC[:, :], [bFC])

            if "C" not in stages:
                continue

            for ti in range(TI):
                emit_bn(FC[:, ti * 1024:(ti + 1) * 1024], [bFC], MVD[:, 2 * ti:2 * ti + 2], bMVD, ti)
                emit_rstd(MVD, bMVD, R3, bR3, ti)
                P.do("ve", lambda v, ti=ti: v.tensor_scalar(
                    FD[:, ti * 1024:(ti + 1) * 1024], FC[:, ti * 1024:(ti + 1) * 1024],
                    MVD[:, 2 * ti:2 * ti + 1], R3[:, ti:ti + 1],
                    op0=ALU.subtract, op1=ALU.mult),
                    reads=[bFC, bMVD, bR3], writes=[bFD])
                emit_quant(FD[:, ti * 1024:(ti + 1) * 1024], QP[ti % 2][:, :], 1024,
                           [bFD], [bQP[ti % 2]])
                P.do("syd", lambda s, ti=ti: s.dma_start(
                    XLF[:, :].rearrange("p (kc t) -> p kc t", t=512)[:, :, ti * 128:(ti + 1) * 128],
                    QP[ti % 2][:, :], transpose=True),
                    reads=[bQP[ti % 2]], writes=bXLFk, cost=1.0)
            tap("qzf", XLF[:, :], list(bXLFk))

            acc = [(FB, bFB), (FD, bFD)]
            for si, (n0, w) in enumerate(NS_SLICES):
                jn = w // 128
                for kc in range(KC):
                    P.do("scd", lambda s, kc=kc, n0=n0, w=w: s.dma_start(
                        QRGU[:, kc * 512:kc * 512 + w], c_qg[kc, :, n0:n0 + w]),
                        reads=[bc_qg], writes=[bQRGU])
                for j in range(jn):
                    for nn in range(2):
                        kcd = (n0 // 128) + j
                        P.do("scd", lambda s, j=j, nn=nn, kcd=kcd: s.dma_start(
                            QRD[:, (j * 2 + nn) * 512:(j * 2 + nn + 1) * 512],
                            c_qd[kcd, :, nn * 512:(nn + 1) * 512]),
                            reads=[bc_qd], writes=[bQRD])
                for ti in range(TI):
                    p = ti % 2
                    pbg, bpbg = PB[4] if p == 0 else PB[0], bPB[4] if p == 0 else bPB[0]
                    for kc in range(KC):
                        P.do("te", lambda t, ti=ti, kc=kc, w=w, pbg=pbg: t.matmul(
                            pbg[:, 0:w], XLF[:, kc * 512 + ti * 128:kc * 512 + (ti + 1) * 128],
                            QRGU[:, kc * 512:kc * 512 + w],
                            start=(kc == 0), stop=(kc == KC - 1)),
                            reads=[bXLFk[kc], bQRGU], writes=[bpbg])
                    P.do("se", lambda s, ti=ti, w=w, pbg=pbg: s.activation(
                        SGQ[:, ti * 512:ti * 512 + w], pbg[:, 0:w], AF.Silu),
                        reads=[bpbg], writes=[bSGQ])
                for kc in range(KC):
                    P.do("scd", lambda s, kc=kc, n0=n0, w=w: s.dma_start(
                        QRGU[:, kc * 512:kc * 512 + w], c_qu[kc, :, n0:n0 + w]),
                        reads=[bc_qu], writes=[bQRGU])
                for ti in range(TI):
                    p = ti % 2
                    pbu, bpbu = PB[5] if p == 0 else PB[1], bPB[5] if p == 0 else bPB[1]
                    pbd = (PB[6], PB[7]) if p == 0 else (PB[2], PB[3])
                    bpbd = (bPB[6], bPB[7]) if p == 0 else (bPB[2], bPB[3])
                    hpp, bhpp = HPPs[p]
                    qph, bqph = QPH[p]
                    for kc in range(KC):
                        P.do("te", lambda t, ti=ti, kc=kc, w=w, pbu=pbu: t.matmul(
                            pbu[:, 0:w], XLF[:, kc * 512 + ti * 128:kc * 512 + (ti + 1) * 128],
                            QRGU[:, kc * 512:kc * 512 + w],
                            start=(kc == 0), stop=(kc == KC - 1)),
                            reads=[bXLFk[kc], bQRGU], writes=[bpbu])
                    P.do("ve", lambda v, ti=ti, w=w, hpp=hpp, pbu=pbu: v.tensor_tensor(
                        hpp[:, 0:w], SGQ[:, ti * 512:ti * 512 + w], pbu[:, 0:w], op=ALU.mult),
                        reads=[bSGQ, bpbu], writes=[bhpp])
                    emit_quant(hpp[:, 0:w], qph[:, 0:w], w, [bhpp], [bqph], qs=QSH[p])
                    qht = QHT[p]
                    P.do("syd", lambda s, qht=qht, qph=qph, jn=jn: s.dma_start(
                        qht[:, 0:jn * 128].rearrange("p (j c) -> p j c", c=128),
                        qph[:, 0:jn * 128], transpose=True),
                        reads=[bqph], writes=[bQHTs[p]], cost=0.8)
                    for j in range(jn):
                        for nn in range(2):
                            P.do("te", lambda t, qht=qht, j=j, nn=nn, jn=jn, pbd=pbd: t.matmul(
                                pbd[nn][:, :], qht[:, j * 128:(j + 1) * 128],
                                QRD[:, (j * 2 + nn) * 512:(j * 2 + nn + 1) * 512],
                                start=(j == 0), stop=(j == jn - 1)),
                                reads=[bQHTs[p], bQRD], writes=[bpbd[nn]])
                    dst, bdst = acc[si % 2]
                    prv, bprv = acc[(si + 1) % 2]
                    for nn in range(2):
                        if si == 0:
                            P.do("ve", lambda v, ti=ti, nn=nn, dst=dst, pbd=pbd: v.tensor_tensor(
                                dst[:, ti * 1024 + nn * 512:ti * 1024 + (nn + 1) * 512],
                                FC[:, ti * 1024 + nn * 512:ti * 1024 + (nn + 1) * 512],
                                pbd[nn][:, :], op=ALU.add),
                                reads=[bFC, bpbd[nn]], writes=[bdst])
                        else:
                            P.do("ve", lambda v, ti=ti, nn=nn, dst=dst, prv=prv, pbd=pbd: v.tensor_tensor(
                                dst[:, ti * 1024 + nn * 512:ti * 1024 + (nn + 1) * 512],
                                prv[:, ti * 1024 + nn * 512:ti * 1024 + (nn + 1) * 512],
                                pbd[nn][:, :], op=ALU.add),
                                reads=[bprv, bpbd[nn]], writes=[bdst])
            bvout = ve_settle([bFD])
            for ti in range(TI):
                P.do("gpd", lambda g, ti=ti, t0=t0: g.dma_start(
                    yo[t0 + ti * 128:t0 + (ti + 1) * 128, :],
                    FD[:, ti * 1024:(ti + 1) * 1024]), reads=[bFD, bvout], writes=[by])

        P.schedule()
        final_waits = [(n, c) for n, c in P.cnt.items()
                       if n in ("B_y", "B_misc") or n.startswith("B_tap_")]

        import os
        log = open("/tmp/replay.log", "w") if os.environ.get("KLOG") else None
        with nc.Block() as block:
            @block.sync
            def _(eng):
                P.replay("sync", eng, log)

            @block.tensor
            def _(eng):
                P.replay("tensor", eng, log)

            @block.vector
            def _(eng):
                P.replay("vector", eng, log)

            @block.scalar
            def _(eng):
                P.replay("scalar", eng, log)

            @block.gpsimd
            def _(eng):
                P.replay("gpsimd", eng, log)
                for n, c in final_waits:
                    eng.wait_ge(P.sems[n], c)
        if log is not None:
            log.close()

    return nc


def _host_inputs(inputs, core):
    f32 = np.float32
    x = np.ascontiguousarray(inputs["L0_attn_x"][2 * core:2 * core + 2], f32).reshape(SEQS * T, C)
    d = {
        "x0": x,
        "wkT": np.ascontiguousarray(np.asarray(inputs["L0_attn_Wk"], f32).T),
        "wvT": np.ascontiguousarray(np.asarray(inputs["L0_attn_Wv"], f32).T),
        "wrT": np.ascontiguousarray(np.asarray(inputs["L0_attn_Wr"], f32).T),
        "woT": np.ascontiguousarray(np.asarray(inputs["L0_attn_Wo"], f32).T),
        "ffkT": np.ascontiguousarray(np.asarray(inputs["L0_attn_Wffk"], f32).T),
        "ffvT": np.ascontiguousarray(np.asarray(inputs["L0_attn_Wffv"], f32).T),
        "wg": np.ascontiguousarray(np.asarray(inputs["L0_ffn_w_gate"], f32)),
        "wu": np.ascontiguousarray(np.asarray(inputs["L0_ffn_w_up"], f32)),
        "wd": np.ascontiguousarray(np.asarray(inputs["L0_ffn_w_down"], f32)),
        "td": np.ascontiguousarray(np.asarray(inputs["L0_attn_time_decay"], f32).reshape(C, 1)),
        "tf": np.ascontiguousarray(np.asarray(inputs["L0_attn_time_first"], f32).reshape(C, 1)),
        "g1": np.ascontiguousarray(np.asarray(inputs["L0_attn_ln1_g"], f32).reshape(C, 1)),
        "b1": np.ascontiguousarray(np.asarray(inputs["L0_attn_ln1_b"], f32).reshape(C, 1)),
        "g2": np.ascontiguousarray(np.asarray(inputs["L0_attn_ln2_g"], f32).reshape(C, 1)),
        "b2": np.ascontiguousarray(np.asarray(inputs["L0_attn_ln2_b"], f32).reshape(C, 1)),
    }
    return d


def kernel(**inputs):
    nc = build_program()
    in_maps = [_host_inputs(inputs, c) for c in range(NCORES)]
    res = run_bass_kernel_spmd(nc, in_maps, core_ids=list(range(NCORES)))
    out = np.empty((2 * NCORES, T, C), np.float32)
    for c in range(NCORES):
        out[2 * c:2 * c + 2] = np.asarray(res.results[c]["y"]).reshape(2, T, C)
    return out


# version 12 (final)
# speedup vs baseline: 1.1785x; 1.1785x over previous
import contextlib
import numpy as np
import ml_dtypes
import concourse.bass as bass
import concourse.mybir as mybir
from concourse.bass_utils import run_bass_kernel_spmd

F32 = mybir.dt.float32
BF16 = mybir.dt.bfloat16
AF = mybir.ActivationFunctionType
ALU = mybir.AluOpType
AX = mybir.AxisListType
bf16 = ml_dtypes.bfloat16

NCORES = 8
SEQS = 2
T = 4096
C = 1024
KC = 8
TC = 512
TI = TC // 128
NCHUNK = SEQS * T // TC
IF = 4096
IFC = IF // 128
G = 2816
GC = G // 128
LN_EPS = 1e-5
WKV_EPS = 1e-6
E30 = float(np.float32(np.exp(30.0)))
M1 = float(np.float32(1.5 * 2**22))
M2 = float(np.float32(1.5 * 2**23))
M3 = float(np.float32(1.5 * 2**24))
NS_SLICES = [(0, 512), (512, 512), (1024, 512), (1536, 512), (2048, 512), (2560, 256)]
PAD_CC = 0
SETTLES = False

ENGINES = ("sync", "tensor", "vector", "scalar", "gpsimd")
STREAM_ENGINE = {"te": "tensor", "ve": "vector", "se": "scalar", "ge": "gpsimd",
                 "syd": "sync", "scd": "scalar", "gpd": "gpsimd"}
STREAM_STEP = {"te": 1, "ve": 1, "se": 1, "ge": 1, "syd": 16, "scd": 16, "gpd": 16}


class Buf:
    __slots__ = ("name", "writer", "readers")

    def __init__(self, name):
        self.name = name
        self.writer = None
        self.readers = []


class Op:
    __slots__ = ("id", "stream", "engine", "sem", "emit", "deps", "label",
                 "cost", "count")

    def __init__(self, id, stream, engine, sem, emit, deps, label, cost):
        self.id = id
        self.stream = stream
        self.engine = engine
        self.sem = sem
        self.emit = emit
        self.deps = deps
        self.label = label
        self.cost = cost
        self.count = None


DEF_COST = {"te": 0.18, "ve": 0.55, "se": 0.45, "ge": 0.9,
            "syd": 1.3, "scd": 1.3, "gpd": 1.3}
DMA_ISSUE = 0.3


class Prog:
    def __init__(self):
        self.ops = []
        self.sems = {}
        self.sem_alloc = None
        self.last_dma_write = {}
        self.sched = None

    def _sem(self, name):
        if name not in self.sems:
            self.sems[name] = self.sem_alloc(name)
        return self.sems[name]

    def do(self, stream, emit, reads=(), writes=(), cost=None):
        is_dma = stream in ("syd", "scd", "gpd")
        oid = len(self.ops)
        if is_dma:
            sem = "B_" + (writes[0].name if writes else "misc")
        else:
            sem = stream
        self._sem(sem)
        deps = set()
        for b in reads:
            if b.writer is not None:
                deps.add(b.writer)
        for b in writes:
            if b.writer is not None:
                w = self.ops[b.writer]
                if not (is_dma and w.sem.startswith("B_") and not b.readers):
                    deps.add(b.writer)
            deps.update(b.readers)
        if is_dma and sem in self.last_dma_write:
            p = self.last_dma_write[sem]
            assert self.ops[p].engine == STREAM_ENGINE[stream], \
                f"buf {sem} written from two queues"
            deps.add(p)
        import sys as _sys
        lbl = f"L{_sys._getframe(1).f_lineno}"
        op = Op(oid, stream, STREAM_ENGINE[stream], sem, emit, deps, lbl,
                DEF_COST[stream] if cost is None else cost)
        self.ops.append(op)
        if is_dma:
            self.last_dma_write[sem] = oid
        for b in reads:
            b.readers.append(oid)
        for b in writes:
            b.writer = oid
            b.readers = []
        return oid

    def schedule(self):
        import heapq
        n = len(self.ops)
        succ = [[] for _ in range(n)]
        indeg = [0] * n
        for op in self.ops:
            for p in op.deps:
                succ[p].append(op.id)
                indeg[op.id] += 1
        prio = [0.0] * n
        for i in range(n - 1, -1, -1):
            m = 0.0
            for s_ in succ[i]:
                if prio[s_] > m:
                    m = prio[s_]
            prio[i] = self.ops[i].cost + m
        npred_done = [0] * n
        ready_t = [0.0] * n
        finish = [0.0] * n
        eng_free = {e: 0.0 for e in ENGINES}
        heap = [(-prio[i], i) for i in range(n) if indeg[i] == 0]
        heapq.heapify(heap)
        order = {e: [] for e in ENGINES}
        scheduled = 0
        while heap:
            _, i = heapq.heappop(heap)
            op = self.ops[i]
            start = max(eng_free[op.engine], ready_t[i])
            if op.sem.startswith("B_"):
                eng_free[op.engine] = start + DMA_ISSUE
                finish[i] = start + DMA_ISSUE + op.cost
            else:
                finish[i] = start + op.cost
                eng_free[op.engine] = finish[i]
            order[op.engine].append(op)
            scheduled += 1
            for s_ in succ[i]:
                npred_done[s_] += 1
                if ready_t[s_] < finish[i]:
                    ready_t[s_] = finish[i]
                if npred_done[s_] == indeg[s_]:
                    heapq.heappush(heap, (-prio[s_], s_))
        assert scheduled == n, f"cycle: scheduled {scheduled}/{n}"
        cnt = {}
        for e in ENGINES:
            for op in order[e]:
                step = 16 if op.sem.startswith("B_") else 1
                cnt[op.sem] = cnt.get(op.sem, 0) + step
                op.count = cnt[op.sem]
        self.sched = order
        self.cnt = cnt
        self.makespan = max(eng_free.values())
        return order

    def replay(self, engine, eng, log=None):
        if self.sched is None:
            self.schedule()
        hw = {}
        for op in self.sched[engine]:
            is_dma = op.stream in ("syd", "scd", "gpd")
            waits = {}
            for p in op.deps:
                po = self.ops[p]
                if po.sem == "te" and op.sem == "te":
                    continue
                if waits.get(po.sem, 0) < po.count:
                    waits[po.sem] = po.count
            prod_engines = set()
            for s, c in sorted(waits.items()):
                if hw.get(s, 0) >= c:
                    continue
                eng.wait_ge(self.sems[s], c)
                if not s.startswith("B_"):
                    prod_engines.add(STREAM_ENGINE[s])
                if log is not None:
                    log.write(f"{engine}: wait {s}>={c}\n")
                hw[s] = c
            if PAD_CC and prod_engines and (is_dma or (prod_engines - {engine})):
                eng.nop(cycle_cnt=PAD_CC)
            inst = op.emit(eng)
            inst.then_inc(self.sems[op.sem], 16 if is_dma else 1)
            if log is not None:
                log.write(f"{engine}: [{op.sem}:{op.count}] {op.label}\n")


def build_program(stages=("A", "B", "C"), n_chunks=NCHUNK, taps=()):
    nc = bass.Bass("TRN2", target_bir_lowering=False)
    P = Prog()

    x0 = nc.dram_tensor("x0", [SEQS * T, C], F32, kind="ExternalInput")
    wkT = nc.dram_tensor("wkT", [C, C], F32, kind="ExternalInput")
    wvT = nc.dram_tensor("wvT", [C, C], F32, kind="ExternalInput")
    wrT = nc.dram_tensor("wrT", [C, C], F32, kind="ExternalInput")
    woT = nc.dram_tensor("woT", [C, C], F32, kind="ExternalInput")
    ffkT = nc.dram_tensor("ffkT", [C, IF], F32, kind="ExternalInput")
    ffvT = nc.dram_tensor("ffvT", [IF, C], F32, kind="ExternalInput")
    wg = nc.dram_tensor("wg", [G, C], F32, kind="ExternalInput")
    wu = nc.dram_tensor("wu", [G, C], F32, kind="ExternalInput")
    wd = nc.dram_tensor("wd", [C, G], F32, kind="ExternalInput")
    td = nc.dram_tensor("td", [C, 1], F32, kind="ExternalInput")
    tfin = nc.dram_tensor("tf", [C, 1], F32, kind="ExternalInput")
    g1d = nc.dram_tensor("g1", [C, 1], F32, kind="ExternalInput")
    b1d = nc.dram_tensor("b1", [C, 1], F32, kind="ExternalInput")
    g2d = nc.dram_tensor("g2", [C, 1], F32, kind="ExternalInput")
    b2d = nc.dram_tensor("b2", [C, 1], F32, kind="ExternalInput")
    yo = nc.dram_tensor("y", [SEQS * T, C], F32, kind="ExternalOutput")
    c_kvr = nc.dram_tensor("c_kvr", [3, KC, KC, 128, 128], BF16)
    c_o = nc.dram_tensor("c_o", [KC, 128, C], BF16)
    c_ffk = nc.dram_tensor("c_ffk", [IFC, KC, 128, 128], BF16)
    c_ffv = nc.dram_tensor("c_ffv", [IFC, 128, C], BF16)
    c_qg = nc.dram_tensor("c_qg", [KC, 128, G], BF16)
    c_qu = nc.dram_tensor("c_qu", [KC, 128, G], BF16)
    c_qd = nc.dram_tensor("c_qd", [GC, 128, C], BF16)
    tap_d = {}
    for tname, tshape, tdt in taps:
        tap_d[tname] = nc.dram_tensor("tap_" + tname, tshape, tdt, kind="ExternalOutput")

    with contextlib.ExitStack() as ctx:
        E = ctx.enter_context

        def sb(name, shape, dt):
            t = E(nc.sbuf_tensor(name, shape, dt))
            return t, Buf(name)

        def ps(name, shape):
            t = E(nc.psum_tensor(name, shape, F32))
            return t, Buf(name)

        FA, bFA = sb("FA", [128, 4096], F32)
        FB, bFB = sb("FB", [128, 4096], F32)
        FC, bFC = sb("FC", [128, 4096], F32)
        FD, bFD = sb("FD", [128, 4096], F32)
        bFAs = [Buf(f"FAs{i}") for i in range(KC)]
        bFBs = [Buf(f"FBs{i}") for i in range(KC)]
        bFCs = [Buf(f"FCs{i}") for i in range(KC)]
        bFDs = [Buf(f"FDs{i}") for i in range(KC)]
        XP, bXP = [], []
        for i in range(2):
            t, b = sb(f"XP{i}", [128, 1024], F32)
            XP.append(t); bXP.append(b)
        SGP, bSGP = sb("SGP", [128, 512], F32)
        HPPs = []
        for i in range(2):
            t, b = sb(f"HPP{i}", [128, 512], F32)
            HPPs.append((t, b))

        def mk_qset(tag, W):
            xnp, bxnp = sb("XNP" + tag, [128, W], F32)
            rndp, brndp = sb("RNDP" + tag, [128, W], F32)
            qt, bqt = [], []
            for i in range(6):
                t, b = sb(f"QT{tag}{i}", [128, W], BF16)
                qt.append(t); bqt.append(b)
            nbm = W // 16
            maxa, bmaxa = sb("MAXA" + tag, [128, nbm], F32)
            scl, bscl = sb("SCL" + tag, [128, nbm], F32)
            rscl, brscl = sb("RSCL" + tag, [128, nbm], F32)
            return dict(XNP=xnp, bXNP=bxnp, RNDP=rndp, bRNDP=brndp,
                        QT=qt, bQT=bqt, MAXA=maxa, bMAXA=bmaxa,
                        SCL=scl, bSCL=bscl, RSCL=rscl, bRSCL=brscl, W=W)

        QS0 = mk_qset("a", 1024)
        QS1 = mk_qset("b", 512)
        QSH = [QS0, QS1]
        XLF, bXLF = sb("XLF", [128, 4096], BF16)
        bXLFk = [Buf(f"XLFk{kc}") for kc in range(KC)]
        SRB, bSRB = sb("SRB", [128, 4096], BF16)
        bSRBs = [Buf(f"SRBs{i}") for i in range(KC)]
        RWKVB, bRWKVB = sb("RWKVB", [128, 4096], BF16)
        bRWs = [Buf(f"RWs{i}") for i in range(KC)]
        RUSQ, bRUSQ = sb("RUSQ", [128, IFC * 512], BF16)
        bRUs = [Buf(f"RUs{i}") for i in range(IFC)]
        QP, bQP = [], []
        for i in range(2):
            t, b = sb(f"QP{i}", [128, 1024], BF16)
            QP.append(t); bQP.append(b)
        QP2, bQP2 = sb("QP2", [128, 512], BF16)
        QPH = [(QP[0], bQP[0]), (QP2, bQP2)]
        QHT, bQHTs = [], []
        for i in range(2):
            t, b = sb(f"QHT{i}", [128, 512], BF16)
            QHT.append(t); bQHTs.append(b)
        QRGU, bQRGU = sb("QRGU", [128, KC * 512], BF16)
        QRD, bQRD = sb("QRD", [128, 8 * 512], BF16)
        SGQ, bSGQ = sb("SGQ", [128, TI * 512], F32)
        LHS, bLHS = [], []
        for i in range(3):
            t, b = sb(f"LHS{i}", [128, 1024], BF16)
            LHS.append(t); bLHS.append(b)
        RHS, bRHS = [], []
        for i in range(4):
            t, b = sb(f"RHS{i}", [128, 512], BF16)
            RHS.append(t); bRHS.append(b)
        BN6, bBN6 = sb("BN6", [128, 12], F32)
        MVA, bMVA = sb("MVA", [128, 8], F32)
        MVB, bMVB = sb("MVB", [128, 8], F32)
        MVCc, bMVC = sb("MVC", [128, 8], F32)
        MVD, bMVD = sb("MVD", [128, 8], F32)
        R0, bR0 = sb("R0", [128, 4], F32)
        R1, bR1 = sb("R1", [128, 4], F32)
        R2, bR2 = sb("R2", [128, 4], F32)
        R3, bR3 = sb("R3", [128, 4], F32)
        TMPS, bTMPS = sb("TMPS", [128, 4], F32)
        G1S, bG1S = sb("G1S", [128, 8], F32)
        B1S, bB1S = sb("B1S", [128, 8], F32)
        G2S, bG2S = sb("G2S", [128, 8], F32)
        B2S, bB2S = sb("B2S", [128, 8], F32)
        B1B, bB1B = sb("B1B", [128, 8], BF16)
        B2B, bB2B = sb("B2B", [128, 8], BF16)
        DEC, bDEC = sb("DEC", [128, 8], F32)
        TFE, bTFE = sb("TFE", [128, 8], F32)
        BK, bBK = sb("BK", [128, 8], F32)
        BV, bBV = sb("BV", [128, 8], F32)
        BR, bBR = sb("BR", [128, 8], F32)
        BU, bBU = sb("BU", [128, 32], F32)
        CNUM, bCNUM = sb("CNUM", [128, 8], F32)
        CDEN, bCDEN = sb("CDEN", [128, 8], F32)
        bCNs = [Buf(f"CNs{i}") for i in range(KC)]
        bCDs = [Buf(f"CDs{i}") for i in range(KC)]
        TTILE, bTTILE = sb("TTILE", [128, 128], BF16)
        CONSTS, bCONSTS = sb("CONSTS", [128, 2], F32)
        if SETTLES:
            ZB, bZB = sb("ZB", [128, 512], BF16)
            SETL, bSETL = sb("SETL", [128, 1], F32)
            SETL2, bSETL2 = sb("SETL2", [128, 1], F32)
        PB, bPB = [], []
        for i in range(8):
            t, b = ps(f"PB{i}", [128, 512])
            PB.append(t); bPB.append(b)
        bx0 = Buf("x0")
        bc_kvr = Buf("c_kvr"); bc_o = Buf("c_o"); bc_ffk = Buf("c_ffk")
        bc_ffv = Buf("c_ffv"); bc_qg = Buf("c_qg"); bc_qu = Buf("c_qu"); bc_qd = Buf("c_qd")
        by = Buf("y")

        P.sem_alloc = lambda name: E(nc.semaphore("sem_" + name))

        def stripe(tile, s, w=512):
            return tile[:, s * w:(s + 1) * w]

        def emit_quant(src_ap, out_ap, W, rbufs, wbufs, qs=None):
            if qs is None:
                qs = QS0
            assert W <= qs["W"]
            nb = W // 16
            XNP, bXNP = qs["XNP"], qs["bXNP"]
            RNDP, bRNDP = qs["RNDP"], qs["bRNDP"]
            QT, bQT = qs["QT"], qs["bQT"]
            MAXA, bMAXA = qs["MAXA"], qs["bMAXA"]
            SCL, bSCL = qs["SCL"], qs["bSCL"]
            RSCL, bRSCL = qs["RSCL"], qs["bRSCL"]
            blk = lambda ap: ap.rearrange("p (nb s) -> p nb s", s=16)
            P.do("ve", lambda v: v.tensor_reduce(
                MAXA[:, 0:nb], blk(src_ap), axis=AX.X, op=ALU.max,
                apply_absolute_value=True), reads=rbufs, writes=[bMAXA])
            P.do("ve", lambda v: v.tensor_scalar(
                SCL[:, 0:nb], MAXA[:, 0:nb], 1.0 / 6.0, 1e-12, op0=ALU.mult, op1=ALU.max),
                reads=[bMAXA], writes=[bSCL], cost=0.1)
            P.do("ve", lambda v: v.reciprocal(RSCL[:, 0:nb], SCL[:, 0:nb]),
                 reads=[bSCL], writes=[bRSCL], cost=0.1)
            rs_b = RSCL[:, 0:nb].unsqueeze(2).broadcast_to([128, nb, 16])
            sc_b = SCL[:, 0:nb].unsqueeze(2).broadcast_to([128, nb, 16])
            P.do("ve", lambda v: v.tensor_tensor(
                blk(XNP[:, 0:W]), blk(src_ap), rs_b, op=ALU.mult),
                reads=rbufs + [bRSCL], writes=[bXNP])
            P.do("ve", lambda v: v.tensor_scalar(
                RNDP[:, 0:W], XNP[:, 0:W], M1, M1 - 2.0, op0=ALU.add, op1=ALU.max),
                reads=[bXNP], writes=[bRNDP])
            P.do("ve", lambda v: v.tensor_scalar(
                QT[0][:, 0:W], RNDP[:, 0:W], M1 + 2.0, M1, op0=ALU.min, op1=ALU.subtract),
                reads=[bRNDP], writes=[bQT[0]])
            P.do("ve", lambda v: v.tensor_scalar(
                RNDP[:, 0:W], XNP[:, 0:W], M2, M2 - 4.0, op0=ALU.add, op1=ALU.max),
                reads=[bXNP], writes=[bRNDP])
            P.do("ve", lambda v: v.tensor_scalar(
                QT[1][:, 0:W], RNDP[:, 0:W], M2 + 4.0, M2, op0=ALU.min, op1=ALU.subtract),
                reads=[bRNDP], writes=[bQT[1]])
            P.do("ge", lambda g: g.tensor_scalar(
                QT[2][:, 0:W], QT[1][:, 0:W], 2.0, -2.0, op0=ALU.min, op1=ALU.max),
                reads=[bQT[1]], writes=[bQT[2]])
            P.do("se", lambda s: s.activation(
                RNDP[:, 0:W], XNP[:, 0:W], AF.Copy, bias=M3),
                reads=[bXNP], writes=[bRNDP])
            P.do("se", lambda s: s.activation(
                QT[3][:, 0:W], RNDP[:, 0:W], AF.Copy, bias=-M3),
                reads=[bRNDP], writes=[bQT[3]])
            P.do("ge", lambda g: g.tensor_scalar(
                QT[4][:, 0:W], QT[3][:, 0:W], 4.0, -4.0, op0=ALU.min, op1=ALU.max),
                reads=[bQT[3]], writes=[bQT[4]])
            P.do("ge", lambda g: g.tensor_tensor(
                QT[5][:, 0:W], QT[1][:, 0:W], QT[2][:, 0:W], op=ALU.subtract),
                reads=[bQT[1], bQT[2]], writes=[bQT[5]])
            P.do("ge", lambda g: g.tensor_tensor(
                QT[2][:, 0:W], QT[3][:, 0:W], QT[4][:, 0:W], op=ALU.subtract),
                reads=[bQT[3], bQT[4]], writes=[bQT[2]])
            P.do("ve", lambda v: v.tensor_tensor(
                QT[3][:, 0:W], QT[0][:, 0:W], QT[5][:, 0:W], op=ALU.add),
                reads=[bQT[0], bQT[5]], writes=[bQT[3]])
            P.do("ve", lambda v: v.tensor_tensor(
                QT[4][:, 0:W], QT[3][:, 0:W], QT[2][:, 0:W], op=ALU.add),
                reads=[bQT[3], bQT[2]], writes=[bQT[4]])
            P.do("ve", lambda v: v.tensor_tensor(
                blk(out_ap), blk(QT[4][:, 0:W]), sc_b, op=ALU.mult),
                reads=[bQT[4], bSCL], writes=wbufs)

        def emit_bn(src_ap, rbufs, mv_ap, bMV, ti):
            P.do("ve", lambda v: v.bn_stats(BN6[:, 0:6], src_ap[:, 0:512]),
                 reads=rbufs, writes=[bBN6])
            P.do("ve", lambda v: v.bn_stats(BN6[:, 6:12], src_ap[:, 512:1024]),
                 reads=rbufs, writes=[bBN6])
            P.do("ve", lambda v: v.bn_aggr(mv_ap, BN6[:, 0:12]),
                 reads=[bBN6], writes=[bMV])

        def emit_rstd(mv, bMV, r_t, bR, ti):
            var_ap = mv[:, 2 * ti + 1:2 * ti + 2]
            P.do("se", lambda s: s.activation(TMPS[:, ti:ti + 1], var_ap, AF.Ln,
                                              bias=CONSTS[:, 0:1]),
                 reads=[bMV, bCONSTS], writes=[bTMPS])
            P.do("se", lambda s: s.activation(r_t[:, ti:ti + 1], TMPS[:, ti:ti + 1], AF.Exp, scale=-0.5),
                 reads=[bTMPS], writes=[bR])

        def mm_settle(junk_idx, acc=False, after_bufs=()):
            if not SETTLES:
                return Buf("dummy")
            for r in range(1):
                P.do("te", lambda t: t.matmul(
                    PB[junk_idx][:, :], ZB[:, 0:128], ZB[:, 0:512],
                    start=not acc, stop=not acc, skip_group_check=True),
                    reads=[bZB] + list(after_bufs), writes=[bPB[junk_idx]])
            return bPB[junk_idx]

        def ve_settle(after_bufs=()):
            if not SETTLES:
                return Buf("dummy")
            P.do("ve", lambda v: v.memset(SETL[:, :], 0.0),
                 reads=list(after_bufs), writes=[bSETL])
            return bSETL

        def se_settle(after_bufs=()):
            if not SETTLES:
                return Buf("dummy")
            P.do("se", lambda s: s.copy(SETL2[:, :], SETL[:, :]),
                 reads=list(after_bufs), writes=[bSETL2])
            return bSETL2

        def tap(name, src_ap, rbufs):
            if name in tap_d:
                dst = tap_d[name]
                P.do("syd", lambda s, d=dst, a=src_ap: s.dma_start(d[:, :], a), reads=rbufs)

        if SETTLES:
            P.do("ve", lambda v: v.memset(ZB[:, :], 0.0), writes=[bZB])
            P.do("ve", lambda v: v.memset(SETL[:, :], 0.0), writes=[bSETL])
        P.do("ve", lambda v: v.memset(CONSTS[:, 0:1], LN_EPS), writes=[bCONSTS])
        P.do("ve", lambda v: v.memset(CONSTS[:, 1:2], WKV_EPS), writes=[bCONSTS])
        for (dr, tile, b) in ((td, DEC, bDEC), (tfin, TFE, bTFE), (g1d, G1S, bG1S),
                              (b1d, B1S, bB1S), (g2d, G2S, bG2S), (b2d, B2S, bB2S)):
            for kc in range(KC):
                P.do("gpd", lambda g, dr=dr, tile=tile, kc=kc: g.dma_start(
                    tile[:, kc:kc + 1], dr[kc * 128:(kc + 1) * 128, 0:1]), writes=[b])
        P.do("se", lambda s: s.activation(DEC[:, :], DEC[:, :], AF.Exp), reads=[bDEC], writes=[bDEC])
        P.do("se", lambda s: s.activation(DEC[:, :], DEC[:, :], AF.Exp, scale=-1.0),
             reads=[bDEC], writes=[bDEC])
        P.do("se", lambda s: s.activation(TFE[:, :], TFE[:, :], AF.Exp), reads=[bTFE], writes=[bTFE])
        P.do("ve", lambda v: v.tensor_copy(B1B[:, :], B1S[:, :]), reads=[bB1S], writes=[bB1B])
        P.do("ve", lambda v: v.tensor_copy(B2B[:, :], B2S[:, :]), reads=[bB2S], writes=[bB2B])

        for qi, (wsrc, bbias) in enumerate(((wkT, bBK), (wvT, bBV), (wrT, bBR))):
            bias_tile = (BK, BV, BR)[qi]
            for kc in range(KC):
                P.do("syd", lambda s, wsrc=wsrc, kc=kc: s.dma_start(
                    FA[:, 0:1024], wsrc[kc * 128:(kc + 1) * 128, :]), writes=[bFA] + bFAs)
                P.do("ve", lambda v, kc=kc: v.tensor_scalar(
                    XLF[:, 0:1024], FA[:, 0:1024], G1S[:, kc:kc + 1], None,
                    op0=ALU.mult, op1=ALU.bypass), reads=[bFA] + bFAs + [bG1S], writes=[bXLF] + bXLFk)
                P.do("ve", lambda v: v.tensor_copy(SRB[:, 0:1024], FA[:, 0:1024]),
                     reads=[bFA] + bFAs, writes=[bSRB] + bSRBs)
                bvs = ve_settle([bSRB, bB1B])
                for m in range(KC):
                    P.do("te", lambda t, m=m, kc=kc: t.matmul(
                        PB[7][:, m:m + 1], SRB[:, m * 128:(m + 1) * 128],
                        B1B[:, kc:kc + 1], start=(kc == 0 and m == 0),
                        stop=(kc == KC - 1 and m == KC - 1)),
                        reads=[bSRB] + bSRBs + [bB1B, bvs], writes=[bPB[7]])
                P.do("syd", lambda s, qi=qi, kc=kc: s.dma_start(
                    bass.AP(c_kvr, ((qi * KC) * KC + kc) * 128 * 128,
                            [[128, 128], [KC * 128 * 128, KC], [1, 128]]),
                    XLF[:, 0:1024].rearrange("p (m c) -> p m c", m=KC)),
                    reads=[bXLF] + bXLFk, writes=[bc_kvr])
            bts = mm_settle(6)
            P.do("ve", lambda v, bias_tile=bias_tile: v.tensor_copy(bias_tile[:, :], PB[7][:, 0:KC]),
                 reads=[bPB[7], bts], writes=[bbias])
        for kc in range(KC):
            P.do("syd", lambda s, kc=kc: s.dma_start(
                FA[:, 0:1024], woT[kc * 128:(kc + 1) * 128, :]), writes=[bFA])
            P.do("ve", lambda v: v.tensor_copy(XLF[:, 0:1024], FA[:, 0:1024]),
                 reads=[bFA] + bFAs, writes=[bXLF] + bXLFk)
            P.do("syd", lambda s, kc=kc: s.dma_start(
                c_o[kc, :, :], XLF[:, 0:1024]), reads=[bXLF] + bXLFk, writes=[bc_o])
        for kc in range(KC):
            P.do("syd", lambda s, kc=kc: s.dma_start(
                FA[:, :], ffkT[kc * 128:(kc + 1) * 128, :]), writes=[bFA])
            P.do("ve", lambda v, kc=kc: v.tensor_scalar(
                XLF[:, :], FA[:, :], G2S[:, kc:kc + 1], None,
                op0=ALU.mult, op1=ALU.bypass), reads=[bFA] + bFAs + [bG2S], writes=[bXLF] + bXLFk)
            P.do("ve", lambda v: v.tensor_copy(SRB[:, :], FA[:, :]), reads=[bFA] + bFAs, writes=[bSRB] + bSRBs)
            bvs2 = ve_settle([bSRB, bB2B])
            for m in range(IFC):
                P.do("te", lambda t, m=m, kc=kc: t.matmul(
                    PB[7][:, m:m + 1], SRB[:, m * 128:(m + 1) * 128],
                    B2B[:, kc:kc + 1], start=(kc == 0 and m == 0),
                    stop=(kc == KC - 1 and m == IFC - 1)),
                    reads=[bSRB] + bSRBs + [bB2B, bvs2], writes=[bPB[7]])
            P.do("syd", lambda s, kc=kc: s.dma_start(
                bass.AP(c_ffk, kc * 128 * 128, [[128, 128], [KC * 128 * 128, IFC], [1, 128]]),
                XLF[:, :].rearrange("p (m c) -> p m c", m=IFC)),
                reads=[bXLF] + bXLFk, writes=[bc_ffk])
        bts2 = mm_settle(6)
        P.do("ve", lambda v: v.tensor_copy(BU[:, :], PB[7][:, 0:IFC]), reads=[bPB[7], bts2], writes=[bBU])
        for kc in range(IFC):
            P.do("syd", lambda s, kc=kc: s.dma_start(
                FA[:, 0:1024], ffvT[kc * 128:(kc + 1) * 128, :]), writes=[bFA] + bFAs)
            P.do("ve", lambda v: v.tensor_copy(XLF[:, 0:1024], FA[:, 0:1024]),
                 reads=[bFA] + bFAs, writes=[bXLF] + bXLFk)
            P.do("syd", lambda s, kc=kc: s.dma_start(
                c_ffv[kc, :, :], XLF[:, 0:1024]), reads=[bXLF] + bXLFk, writes=[bc_ffv])
        for (wsrc, cdst, bdst, mt_n, kcn) in ((wg, c_qg, bc_qg, GC, KC),
                                              (wu, c_qu, bc_qu, GC, KC)):
            for mt in range(mt_n):
                P.do("syd", lambda s, wsrc=wsrc, mt=mt: s.dma_start(
                    FA[:, 0:1024], wsrc[mt * 128:(mt + 1) * 128, :]), writes=[bFA] + bFAs)
                emit_quant(FA[:, 0:1024], XLF[:, 0:1024], 1024, [bFA] + bFAs, [bXLF] + bXLFk)
                for kc in range(kcn):
                    P.do("scd", lambda s, kc=kc: s.dma_start(
                        TTILE[:, :], XLF[:, kc * 128:(kc + 1) * 128], transpose=True),
                        reads=[bXLF] + bXLFk, writes=[bTTILE])
                    P.do("scd", lambda s, cdst=cdst, kc=kc, mt=mt: s.dma_start(
                        cdst[kc, :, mt * 128:(mt + 1) * 128], TTILE[:, :]),
                        reads=[bTTILE], writes=[bdst])
        for mt in range(KC):
            P.do("syd", lambda s, mt=mt: s.dma_start(
                FA[:, 0:G], wd[mt * 128:(mt + 1) * 128, :]), writes=[bFA] + bFAs)
            for qtr in range(4):
                emit_quant(FA[:, qtr * 704:(qtr + 1) * 704], XLF[:, qtr * 704:(qtr + 1) * 704],
                           704, [bFA] + bFAs, [bXLF] + bXLFk)
            for kc in range(GC):
                P.do("scd", lambda s, kc=kc: s.dma_start(
                    TTILE[:, :], XLF[:, kc * 128:(kc + 1) * 128], transpose=True),
                    reads=[bXLF] + bXLFk, writes=[bTTILE])
                P.do("scd", lambda s, kc=kc, mt=mt: s.dma_start(
                    c_qd[kc, :, mt * 128:(mt + 1) * 128], TTILE[:, :]),
                    reads=[bTTILE], writes=[bc_qd])
        for ch in range(n_chunks):
            seq = ch // (T // TC)
            lc = ch % (T // TC)
            t0 = seq * T + lc * TC

            for ti in range(TI):
                xp, bxp = XP[ti % 2], bXP[ti % 2]
                P.do("gpd", lambda g, ti=ti, t0=t0, xp=xp: g.dma_start(
                    xp[:, :], x0[t0 + ti * 128:t0 + (ti + 1) * 128, :]), writes=[bxp])
                emit_bn(xp, [bxp], MVA[:, 2 * ti:2 * ti + 2], bMVA, ti)
                emit_rstd(MVA, bMVA, R0, bR0, ti)
                P.do("se", lambda s, ti=ti: s.activation(
                    MVB[:, 2 * ti:2 * ti + 1], TMPS[:, ti:ti + 1], AF.Exp, scale=-1.0),
                    reads=[bTMPS], writes=[bMVB], cost=0.1)
                P.do("ve", lambda v, ti=ti: v.tensor_tensor(
                    MVB[:, 2 * ti + 1:2 * ti + 2], MVA[:, 2 * ti + 1:2 * ti + 2],
                    MVB[:, 2 * ti:2 * ti + 1], op=ALU.mult),
                    reads=[bMVA, bMVB], writes=[bMVB], cost=0.1)
                P.do("se", lambda s, ti=ti: s.activation(
                    TMPS[:, ti:ti + 1], MVB[:, 2 * ti + 1:2 * ti + 2], AF.Ln,
                    bias=CONSTS[:, 0:1]), reads=[bMVB, bCONSTS], writes=[bTMPS], cost=0.1)
                P.do("se", lambda s, ti=ti: s.activation(
                    R1[:, ti:ti + 1], TMPS[:, ti:ti + 1], AF.Exp, scale=-0.5),
                    reads=[bTMPS], writes=[bR1], cost=0.1)
                P.do("ve", lambda v, ti=ti: v.tensor_tensor(
                    MVB[:, 2 * ti:2 * ti + 1], R0[:, ti:ti + 1], R1[:, ti:ti + 1],
                    op=ALU.mult), reads=[bR0, bR1], writes=[bMVB], cost=0.1)
                P.do("ve", lambda v, ti=ti, xp=xp: v.tensor_scalar(
                    QP[ti % 2][:, :], xp[:, :],
                    MVA[:, 2 * ti:2 * ti + 1], MVB[:, 2 * ti:2 * ti + 1],
                    op0=ALU.subtract, op1=ALU.mult),
                    reads=[bxp, bMVA, bMVB], writes=[bQP[ti % 2]])
                P.do("syd", lambda s, ti=ti: s.dma_start(
                    XLF[:, :].rearrange("p (kc t) -> p kc t", t=512)[:, :, ti * 128:(ti + 1) * 128],
                    QP[ti % 2][:, :], transpose=True),
                    reads=[bQP[ti % 2]], writes=bXLFk, cost=1.0)
            tap("xlf", XLF[:, :], list(bXLFk))

            for qi in range(3):
                bias_tile, bbias = ((BK, bBK), (BV, bBV), (BR, bBR))[qi]
                for m in range(KC):
                    sl = (qi * KC + m) % 3
                    P.do("scd", lambda s, qi=qi, m=m, sl=sl: s.dma_start(
                        LHS[sl][:, :].rearrange("p (a b) -> p a b", a=KC), bass.AP(
                            c_kvr, ((qi * KC + m) * KC) * 128 * 128,
                            [[128, 128], [128 * 128, KC], [1, 128]])),
                        reads=[bc_kvr], writes=[bLHS[sl]])
                    pb, bpb = PB[m % 4], bPB[m % 4]
                    for kc in range(KC):
                        P.do("te", lambda t, pb=pb, sl=sl, kc=kc: t.matmul(
                            pb[:, :], LHS[sl][:, kc * 128:(kc + 1) * 128],
                            XLF[:, kc * 512:kc * 512 + 512],
                            start=(kc == 0), stop=(kc == KC - 1)),
                            reads=[bLHS[sl], bXLFk[kc]], writes=[bpb])
                    extra = []
                    if qi == 2 and m == KC - 1:
                        extra = [mm_settle(2)]
                    if qi == 0:
                        P.do("se", lambda s, m=m, pb=pb: s.activation(
                            stripe(FA, m), pb[:, :], AF.Exp, bias=BK[:, m:m + 1]),
                            reads=[bpb, bBK] + extra, writes=[bFAs[m]])
                    elif qi == 1:
                        P.do("ve", lambda v, m=m, pb=pb: v.tensor_scalar(
                            stripe(FB, m), pb[:, :], BV[:, m:m + 1], None,
                            op0=ALU.add, op1=ALU.bypass),
                            reads=[bpb, bBV] + extra, writes=[bFBs[m]])
                    else:
                        P.do("se", lambda s, m=m, pb=pb: s.activation(
                            stripe(SRB, m), pb[:, :], AF.Sigmoid, bias=BR[:, m:m + 1]),
                            reads=[bpb, bBR] + extra, writes=[bSRBs[m]])
            tap("ek", FA[:, :], list(bFAs))
            tap("vv", FB[:, :], list(bFBs))

            for s_ in range(KC):
                P.do("ve", lambda v, s_=s_: v.tensor_scalar(
                    stripe(FC, s_), stripe(FA, s_), E30, None, op0=ALU.min, op1=ALU.bypass),
                    reads=[bFAs[s_]], writes=[bFCs[s_]])
                P.do("ve", lambda v, s_=s_: v.tensor_tensor(
                    stripe(FD, s_), stripe(FC, s_), stripe(FB, s_), op=ALU.mult),
                    reads=[bFCs[s_], bFBs[s_]], writes=[bFDs[s_]])
                if lc == 0:
                    P.do("ve", lambda v, s_=s_: v.tensor_scalar(
                        FD[:, s_ * 512:s_ * 512 + 1], FD[:, s_ * 512:s_ * 512 + 1],
                        TFE[:, s_:s_ + 1], None, op0=ALU.mult, op1=ALU.bypass),
                        reads=[bFDs[s_], bTFE], writes=[bFDs[s_]], cost=0.1)
                    P.do("ve", lambda v, s_=s_: v.tensor_scalar(
                        FC[:, s_ * 512:s_ * 512 + 1], FC[:, s_ * 512:s_ * 512 + 1],
                        TFE[:, s_:s_ + 1], None, op0=ALU.mult, op1=ALU.bypass),
                        reads=[bFCs[s_], bTFE], writes=[bFCs[s_]], cost=0.1)
                    init_n = 0.0
                    init_d = 0.0
                else:
                    init_n = CNUM[:, s_:s_ + 1]
                    init_d = CDEN[:, s_:s_ + 1]
                dec_b = DEC[:, s_:s_ + 1].broadcast_to([128, 512])
                P.do("ve", lambda v, s_=s_, dec_b=dec_b, init_n=init_n: v.tensor_tensor_scan(
                    stripe(FA, s_), dec_b, stripe(FD, s_), init_n, op0=ALU.mult, op1=ALU.add),
                    reads=[bFDs[s_], bDEC, bCNs[s_]], writes=[bFAs[s_]], cost=1.2)
                P.do("ve", lambda v, s_=s_, dec_b=dec_b, init_d=init_d: v.tensor_tensor_scan(
                    stripe(FB, s_), dec_b, stripe(FC, s_), init_d, op0=ALU.mult, op1=ALU.add),
                    reads=[bFCs[s_], bDEC, bCDs[s_]], writes=[bFBs[s_]], cost=1.2)
                if lc != (T // TC) - 1:
                    P.do("ve", lambda v, s_=s_: v.tensor_copy(
                        CNUM[:, s_:s_ + 1], FA[:, s_ * 512 + 511:s_ * 512 + 512]),
                        reads=[bFAs[s_]], writes=[bCNs[s_]], cost=0.1)
                    P.do("ve", lambda v, s_=s_: v.tensor_copy(
                        CDEN[:, s_:s_ + 1], FB[:, s_ * 512 + 511:s_ * 512 + 512]),
                        reads=[bFBs[s_]], writes=[bCDs[s_]], cost=0.1)
                P.do("se", lambda s, s_=s_: s.activation(
                    stripe(FC, s_), stripe(FB, s_), AF.Ln, bias=CONSTS[:, 1:2]),
                    reads=[bFBs[s_], bCONSTS], writes=[bFCs[s_]])
                P.do("se", lambda s, s_=s_: s.activation(
                    stripe(FD, s_), stripe(FC, s_), AF.Exp, scale=-1.0),
                    reads=[bFCs[s_]], writes=[bFDs[s_]])
                P.do("ve", lambda v, s_=s_: v.tensor_tensor(
                    stripe(FC, s_), stripe(FA, s_), stripe(FD, s_), op=ALU.mult),
                    reads=[bFAs[s_], bFDs[s_]], writes=[bFCs[s_]])
                P.do("ve", lambda v, s_=s_: v.tensor_tensor(
                    stripe(RWKVB, s_), stripe(FC, s_), stripe(SRB, s_), op=ALU.mult),
                    reads=[bFCs[s_], bSRBs[s_]], writes=[bRWs[s_]])
            bvrwkv = ve_settle(bRWs)
            tap("wkv", FC[:, :], list(bFCs))
            tap("rwkv", RWKVB[:, :], list(bRWs))

            if "B" not in stages:
                continue

            for ti in range(TI):
                xp, bxp = XP[ti % 2], bXP[ti % 2]
                P.do("gpd", lambda g, ti=ti, t0=t0, xp=xp: g.dma_start(
                    xp[:, :], x0[t0 + ti * 128:t0 + (ti + 1) * 128, :]), writes=[bxp])
                P.do("ve", lambda v, ti=ti, xp=xp: v.tensor_scalar(
                    FD[:, ti * 1024:(ti + 1) * 1024], xp[:, :],
                    MVA[:, 2 * ti:2 * ti + 1], R0[:, ti:ti + 1],
                    op0=ALU.subtract, op1=ALU.mult),
                    reads=[bxp, bMVA, bR0], writes=[bFDs[2 * ti], bFDs[2 * ti + 1]])
            tap("yd", FD[:, :], list(bFDs))
            for nn in range(2):
                reg, breg = (QRGU, bQRGU) if nn == 0 else (QRD, bQRD)
                for kc in range(KC):
                    P.do("scd", lambda s, kc=kc, nn=nn, reg=reg: s.dma_start(
                        reg[:, kc * 512:(kc + 1) * 512], c_o[kc, :, nn * 512:(nn + 1) * 512]),
                        reads=[bc_o], writes=[breg])
            for nn in range(2):
                pbase = 4 * nn
                reg, breg = (QRGU, bQRGU) if nn == 0 else (QRD, bQRD)
                for kc in range(KC):
                    for ti in range(TI):
                        P.do("te", lambda t, ti=ti, kc=kc, pbase=pbase, reg=reg, breg=breg: t.matmul(
                            PB[pbase + ti][:, :],
                            RWKVB[:, kc * 512 + ti * 128:kc * 512 + (ti + 1) * 128],
                            reg[:, kc * 512:(kc + 1) * 512],
                            start=(kc == 0), stop=(kc == KC - 1)),
                            reads=[bRWs[kc], breg, bvrwkv], writes=[bPB[pbase + ti]])
            bos = mm_settle(7, acc=True)
            for nn in range(2):
                pbase = 4 * nn
                for ti in range(TI):
                    P.do("ve", lambda v, ti=ti, nn=nn, pbase=pbase: v.tensor_tensor(
                        FC[:, ti * 1024 + nn * 512:ti * 1024 + (nn + 1) * 512],
                        FD[:, ti * 1024 + nn * 512:ti * 1024 + (nn + 1) * 512],
                        PB[pbase + ti][:, :], op=ALU.add),
                        reads=[bFDs[2 * ti + nn], bPB[pbase + ti], bos],
                        writes=[bFCs[2 * ti + nn]])
            tap("a", FC[:, :], list(bFCs))

            for ti in range(TI):
                emit_bn(FC[:, ti * 1024:(ti + 1) * 1024], [bFCs[2 * ti], bFCs[2 * ti + 1]], MVCc[:, 2 * ti:2 * ti + 2], bMVC, ti)
                emit_rstd(MVCc, bMVC, R2, bR2, ti)
                P.do("ve", lambda v, ti=ti: v.tensor_scalar(
                    QP[ti % 2][:, :], FC[:, ti * 1024:(ti + 1) * 1024],
                    MVCc[:, 2 * ti:2 * ti + 1], R2[:, ti:ti + 1],
                    op0=ALU.subtract, op1=ALU.mult),
                    reads=[bFCs[2 * ti], bFCs[2 * ti + 1], bMVC, bR2], writes=[bQP[ti % 2]])
                P.do("syd", lambda s, ti=ti: s.dma_start(
                    XLF[:, :].rearrange("p (kc t) -> p kc t", t=512)[:, :, ti * 128:(ti + 1) * 128],
                    QP[ti % 2][:, :], transpose=True),
                    reads=[bQP[ti % 2]], writes=bXLFk, cost=1.0)

            for m in range(IFC):
                sl = m % 3
                P.do("scd", lambda s, m=m, sl=sl: s.dma_start(
                    LHS[sl][:, :].rearrange("p (a b) -> p a b", a=KC), bass.AP(
                        c_ffk, (m * KC) * 128 * 128,
                        [[128, 128], [128 * 128, KC], [1, 128]])),
                    reads=[bc_ffk], writes=[bLHS[sl]])
                pb, bpb = PB[4 + m % 4], bPB[4 + m % 4]
                for kc in range(KC):
                    P.do("te", lambda t, pb=pb, sl=sl, kc=kc: t.matmul(
                        pb[:, :], LHS[sl][:, kc * 128:(kc + 1) * 128],
                        XLF[:, kc * 512:kc * 512 + 512],
                        start=(kc == 0), stop=(kc == KC - 1)),
                        reads=[bLHS[sl], bXLFk[kc]], writes=[bpb])
                extra_u = [mm_settle(0)] if m == IFC - 1 else []
                P.do("se", lambda s, m=m, pb=pb: s.activation(
                    SGP[:, :], pb[:, :], AF.Relu, bias=BU[:, m:m + 1]),
                    reads=[bpb, bBU] + extra_u, writes=[bSGP])
                P.do("se", lambda s, m=m: s.activation(
                    stripe(RUSQ, m), SGP[:, :], AF.Square),
                    reads=[bSGP], writes=[bRUs[m]])
            bsrusq = se_settle(bRUs)
            for ti in range(TI):
                xp = FB
                P.do("gpd", lambda g, ti=ti, t0=t0: g.dma_start(
                    FB[:, ti * 1024:(ti + 1) * 1024],
                    x0[t0 + ti * 128:t0 + (ti + 1) * 128, :]),
                    writes=[bFBs[2 * ti], bFBs[2 * ti + 1]])
            for nn in range(2):
                for kc in range(IFC):
                    sl = kc % 4
                    P.do("scd", lambda s, kc=kc, nn=nn, sl=sl: s.dma_start(
                        RHS[sl][:, :], c_ffv[kc, :, nn * 512:(nn + 1) * 512]),
                        reads=[bc_ffv], writes=[bRHS[sl]])
                    for ti in range(TI):
                        P.do("te", lambda t, ti=ti, kc=kc, sl=sl: t.matmul(
                            PB[ti][:, :], RUSQ[:, kc * 512 + ti * 128:kc * 512 + (ti + 1) * 128],
                            RHS[sl][:, :], start=(kc == 0), stop=(kc == IFC - 1)),
                            reads=[bRUs[kc], bRHS[sl], bsrusq], writes=[bPB[ti]])
                bfs = mm_settle(4) if nn == 1 else mm_settle(5)
                for ti in range(TI):
                    P.do("ve", lambda v, ti=ti, nn=nn: v.tensor_tensor(
                        FA[:, ti * 1024 + nn * 512:ti * 1024 + (nn + 1) * 512],
                        FC[:, ti * 1024 + nn * 512:ti * 1024 + (nn + 1) * 512],
                        PB[ti][:, :], op=ALU.add),
                        reads=[bFCs[2 * ti + nn], bPB[ti], bfs],
                        writes=[bFAs[2 * ti + nn]])
                    P.do("ve", lambda v, ti=ti, nn=nn: v.tensor_tensor(
                        FC[:, ti * 1024 + nn * 512:ti * 1024 + (nn + 1) * 512],
                        FA[:, ti * 1024 + nn * 512:ti * 1024 + (nn + 1) * 512],
                        FB[:, ti * 1024 + nn * 512:ti * 1024 + (nn + 1) * 512],
                        op=ALU.add),
                        reads=[bFAs[2 * ti + nn], bFBs[2 * ti + nn]],
                        writes=[bFCs[2 * ti + nn]])
            tap("x1", FC[:, :], list(bFCs))

            if "C" not in stages:
                continue

            for ti in range(TI):
                emit_bn(FC[:, ti * 1024:(ti + 1) * 1024], [bFCs[2 * ti], bFCs[2 * ti + 1]], MVD[:, 2 * ti:2 * ti + 2], bMVD, ti)
                emit_rstd(MVD, bMVD, R3, bR3, ti)
                P.do("ve", lambda v, ti=ti: v.tensor_scalar(
                    FD[:, ti * 1024:(ti + 1) * 1024], FC[:, ti * 1024:(ti + 1) * 1024],
                    MVD[:, 2 * ti:2 * ti + 1], R3[:, ti:ti + 1],
                    op0=ALU.subtract, op1=ALU.mult),
                    reads=[bFCs[2 * ti], bFCs[2 * ti + 1], bMVD, bR3],
                    writes=[bFDs[2 * ti], bFDs[2 * ti + 1]])
                emit_quant(FD[:, ti * 1024:(ti + 1) * 1024], QP[ti % 2][:, :], 1024,
                           [bFDs[2 * ti], bFDs[2 * ti + 1]], [bQP[ti % 2]])
                P.do("syd", lambda s, ti=ti: s.dma_start(
                    XLF[:, :].rearrange("p (kc t) -> p kc t", t=512)[:, :, ti * 128:(ti + 1) * 128],
                    QP[ti % 2][:, :], transpose=True),
                    reads=[bQP[ti % 2]], writes=bXLFk, cost=1.0)
            tap("qzf", XLF[:, :], list(bXLFk))

            acc = [(FB, bFBs), (FD, bFDs)]
            for si, (n0, w) in enumerate(NS_SLICES):
                jn = w // 128
                for kc in range(KC):
                    P.do("scd", lambda s, kc=kc, n0=n0, w=w: s.dma_start(
                        QRGU[:, kc * 512:kc * 512 + w], c_qg[kc, :, n0:n0 + w]),
                        reads=[bc_qg], writes=[bQRGU])
                for j in range(jn):
                    for nn in range(2):
                        kcd = (n0 // 128) + j
                        P.do("scd", lambda s, j=j, nn=nn, kcd=kcd: s.dma_start(
                            QRD[:, (j * 2 + nn) * 512:(j * 2 + nn + 1) * 512],
                            c_qd[kcd, :, nn * 512:(nn + 1) * 512]),
                            reads=[bc_qd], writes=[bQRD])
                for ti in range(TI):
                    p = ti % 2
                    pbg, bpbg = PB[4] if p == 0 else PB[0], bPB[4] if p == 0 else bPB[0]
                    for kc in range(KC):
                        P.do("te", lambda t, ti=ti, kc=kc, w=w, pbg=pbg: t.matmul(
                            pbg[:, 0:w], XLF[:, kc * 512 + ti * 128:kc * 512 + (ti + 1) * 128],
                            QRGU[:, kc * 512:kc * 512 + w],
                            start=(kc == 0), stop=(kc == KC - 1)),
                            reads=[bXLFk[kc], bQRGU], writes=[bpbg])
                    P.do("se", lambda s, ti=ti, w=w, pbg=pbg: s.activation(
                        SGQ[:, ti * 512:ti * 512 + w], pbg[:, 0:w], AF.Silu),
                        reads=[bpbg], writes=[bSGQ])
                for kc in range(KC):
                    P.do("scd", lambda s, kc=kc, n0=n0, w=w: s.dma_start(
                        QRGU[:, kc * 512:kc * 512 + w], c_qu[kc, :, n0:n0 + w]),
                        reads=[bc_qu], writes=[bQRGU])
                for ti in range(TI):
                    p = ti % 2
                    pbu, bpbu = PB[5] if p == 0 else PB[1], bPB[5] if p == 0 else bPB[1]
                    pbd = (PB[6], PB[7]) if p == 0 else (PB[2], PB[3])
                    bpbd = (bPB[6], bPB[7]) if p == 0 else (bPB[2], bPB[3])
                    hpp, bhpp = HPPs[p]
                    qph, bqph = QPH[p]
                    for kc in range(KC):
                        P.do("te", lambda t, ti=ti, kc=kc, w=w, pbu=pbu: t.matmul(
                            pbu[:, 0:w], XLF[:, kc * 512 + ti * 128:kc * 512 + (ti + 1) * 128],
                            QRGU[:, kc * 512:kc * 512 + w],
                            start=(kc == 0), stop=(kc == KC - 1)),
                            reads=[bXLFk[kc], bQRGU], writes=[bpbu])
                    P.do("ve", lambda v, ti=ti, w=w, hpp=hpp, pbu=pbu: v.tensor_tensor(
                        hpp[:, 0:w], SGQ[:, ti * 512:ti * 512 + w], pbu[:, 0:w], op=ALU.mult),
                        reads=[bSGQ, bpbu], writes=[bhpp])
                    emit_quant(hpp[:, 0:w], qph[:, 0:w], w, [bhpp], [bqph], qs=QSH[p])
                    qht = QHT[p]
                    P.do("syd", lambda s, qht=qht, qph=qph, jn=jn: s.dma_start(
                        qht[:, 0:jn * 128].rearrange("p (j c) -> p j c", c=128),
                        qph[:, 0:jn * 128], transpose=True),
                        reads=[bqph], writes=[bQHTs[p]], cost=0.8)
                    for j in range(jn):
                        for nn in range(2):
                            P.do("te", lambda t, qht=qht, j=j, nn=nn, jn=jn, pbd=pbd: t.matmul(
                                pbd[nn][:, :], qht[:, j * 128:(j + 1) * 128],
                                QRD[:, (j * 2 + nn) * 512:(j * 2 + nn + 1) * 512],
                                start=(j == 0), stop=(j == jn - 1)),
                                reads=[bQHTs[p], bQRD], writes=[bpbd[nn]])
                    dst, bdst = acc[si % 2]
                    prv, bprv = acc[(si + 1) % 2]
                    for nn in range(2):
                        if si == 0:
                            P.do("ve", lambda v, ti=ti, nn=nn, dst=dst, pbd=pbd: v.tensor_tensor(
                                dst[:, ti * 1024 + nn * 512:ti * 1024 + (nn + 1) * 512],
                                FC[:, ti * 1024 + nn * 512:ti * 1024 + (nn + 1) * 512],
                                pbd[nn][:, :], op=ALU.add),
                                reads=[bFCs[2 * ti + nn], bpbd[nn]],
                                writes=[bdst[2 * ti + nn]])
                        else:
                            P.do("ve", lambda v, ti=ti, nn=nn, dst=dst, prv=prv, pbd=pbd: v.tensor_tensor(
                                dst[:, ti * 1024 + nn * 512:ti * 1024 + (nn + 1) * 512],
                                prv[:, ti * 1024 + nn * 512:ti * 1024 + (nn + 1) * 512],
                                pbd[nn][:, :], op=ALU.add),
                                reads=[bprv[2 * ti + nn], bpbd[nn]],
                                writes=[bdst[2 * ti + nn]])
            bvout = ve_settle(bFDs)
            for ti in range(TI):
                P.do("gpd", lambda g, ti=ti, t0=t0: g.dma_start(
                    yo[t0 + ti * 128:t0 + (ti + 1) * 128, :],
                    FD[:, ti * 1024:(ti + 1) * 1024]),
                    reads=[bFDs[2 * ti], bFDs[2 * ti + 1], bvout], writes=[by])

        P.schedule()
        final_waits = [(n, c) for n, c in P.cnt.items()
                       if n in ("B_y", "B_misc") or n.startswith("B_tap_")]

        import os
        log = open("/tmp/replay.log", "w") if os.environ.get("KLOG") else None
        with nc.Block() as block:
            @block.sync
            def _(eng):
                P.replay("sync", eng, log)

            @block.tensor
            def _(eng):
                P.replay("tensor", eng, log)

            @block.vector
            def _(eng):
                P.replay("vector", eng, log)

            @block.scalar
            def _(eng):
                P.replay("scalar", eng, log)

            @block.gpsimd
            def _(eng):
                P.replay("gpsimd", eng, log)
                for n, c in final_waits:
                    eng.wait_ge(P.sems[n], c)
        if log is not None:
            log.close()

    return nc


def _host_inputs(inputs, core):
    f32 = np.float32
    x = np.ascontiguousarray(inputs["L0_attn_x"][2 * core:2 * core + 2], f32).reshape(SEQS * T, C)
    d = {
        "x0": x,
        "wkT": np.ascontiguousarray(np.asarray(inputs["L0_attn_Wk"], f32).T),
        "wvT": np.ascontiguousarray(np.asarray(inputs["L0_attn_Wv"], f32).T),
        "wrT": np.ascontiguousarray(np.asarray(inputs["L0_attn_Wr"], f32).T),
        "woT": np.ascontiguousarray(np.asarray(inputs["L0_attn_Wo"], f32).T),
        "ffkT": np.ascontiguousarray(np.asarray(inputs["L0_attn_Wffk"], f32).T),
        "ffvT": np.ascontiguousarray(np.asarray(inputs["L0_attn_Wffv"], f32).T),
        "wg": np.ascontiguousarray(np.asarray(inputs["L0_ffn_w_gate"], f32)),
        "wu": np.ascontiguousarray(np.asarray(inputs["L0_ffn_w_up"], f32)),
        "wd": np.ascontiguousarray(np.asarray(inputs["L0_ffn_w_down"], f32)),
        "td": np.ascontiguousarray(np.asarray(inputs["L0_attn_time_decay"], f32).reshape(C, 1)),
        "tf": np.ascontiguousarray(np.asarray(inputs["L0_attn_time_first"], f32).reshape(C, 1)),
        "g1": np.ascontiguousarray(np.asarray(inputs["L0_attn_ln1_g"], f32).reshape(C, 1)),
        "b1": np.ascontiguousarray(np.asarray(inputs["L0_attn_ln1_b"], f32).reshape(C, 1)),
        "g2": np.ascontiguousarray(np.asarray(inputs["L0_attn_ln2_g"], f32).reshape(C, 1)),
        "b2": np.ascontiguousarray(np.asarray(inputs["L0_attn_ln2_b"], f32).reshape(C, 1)),
    }
    return d


def kernel(**inputs):
    nc = build_program()
    in_maps = [_host_inputs(inputs, c) for c in range(NCORES)]
    res = run_bass_kernel_spmd(nc, in_maps, core_ids=list(range(NCORES)))
    out = np.empty((2 * NCORES, T, C), np.float32)
    for c in range(NCORES):
        out[2 * c:2 * c + 2] = np.asarray(res.results[c]["y"]).reshape(2, T, C)
    return out
